# version 13
# baseline (speedup 1.0000x reference)
import numpy as np
from collections import defaultdict
import concourse.bass as bass
import concourse.mybir as mybir
from concourse.bass_utils import run_bass_kernel_spmd
from contextlib import ExitStack

F32 = mybir.dt.float32
BF16 = mybir.dt.bfloat16
I32 = mybir.dt.int32
AF = mybir.ActivationFunctionType
ALU = mybir.AluOpType
AX = mybir.AxisListType

ENGINES = ("sync", "tensor", "scalar", "vector", "gpsimd")

D = 2048
NCH = D // 128
H = 16
DH = 128
HID = 5632
QK_SCALE = 1.0 / np.sqrt(np.sqrt(DH))
EPS = 1e-5


class Prog:

    def __init__(self, nc, ctx):
        self.nc = nc
        self.ctx = ctx
        self.C = defaultdict(int)
        self.sems = {}
        self.cur = None
        self.E = None

    def sem(self, name):
        if name not in self.sems:
            self.sems[name] = self.ctx.enter_context(self.nc.semaphore(name))
        self.C.setdefault(name, 0)
        return self.sems[name]

    def phase(self, name, fn):
        snap = dict(self.C)
        final = None
        with self.nc.Block(name) as blk:
            for eng in ENGINES:
                self.C = defaultdict(int, snap)
                self.cur = eng

                def run(E, fn=fn):
                    self.E = E
                    fn(self)

                getattr(blk, eng)(run)
                if final is None:
                    final = dict(self.C)
                else:
                    assert final == dict(self.C), f"phase {name}: nondeterministic emission ({eng})"
        self.C = defaultdict(int, final)

    def wait(self, eng, sem, val=None):
        self.sem(sem)
        if val is None:
            val = self.C[sem]
        if val <= 0:
            return
        if self.cur == eng:
            self.E.wait_ge(self.sems[sem], val)

    def op(self, eng, emit, inc=None, inc_n=1):
        r = None
        if self.cur == eng:
            r = emit(self.E)
        if inc is not None:
            self.sem(inc)
            self.C[inc] += inc_n
            if r is not None:
                r.then_inc(self.sems[inc], inc_n)
        return r

    def dma(self, eng, dst, src, inc):
        self.op(eng, lambda E: E.dma_start(dst, src), inc=inc, inc_n=16)


def build_program(S=4096, hid=HID, debug_stop=None):
    nhc = hid // 128
    S_half = S // 2
    nqt = S_half // 512
    nqc = S_half // 128
    assert S % 1024 == 0

    nc = bass.Bass("TRN2", target_bir_lowering=False)

    xt_d = nc.dram_tensor("xt", [D, S], F32, kind="ExternalInput")
    xtok_d = nc.dram_tensor("xtok", [S, D], F32, kind="ExternalInput")
    wqt_d = nc.dram_tensor("wqt", [D, D], F32, kind="ExternalInput")
    wkt_d = nc.dram_tensor("wkt", [D, D], F32, kind="ExternalInput")
    wvt_d = nc.dram_tensor("wvt", [D, D], F32, kind="ExternalInput")
    wot_d = nc.dram_tensor("wot", [D, D], F32, kind="ExternalInput")
    bq_d = nc.dram_tensor("bq", [128, NCH], F32, kind="ExternalInput")
    bk_d = nc.dram_tensor("bk", [128, NCH], F32, kind="ExternalInput")
    bv_d = nc.dram_tensor("bv", [128, NCH], F32, kind="ExternalInput")
    bo_d = nc.dram_tensor("bo_row", [1, D], F32, kind="ExternalInput")
    wgt_d = nc.dram_tensor("wgt_r", [hid // 128, 128, D], I32, kind="ExternalInput")
    wut_d = nc.dram_tensor("wut_r", [hid // 128, 128, D], I32, kind="ExternalInput")
    wdt_d = nc.dram_tensor("wdt", [hid, D], I32, kind="ExternalInput")
    sg_d = nc.dram_tensor("sg_row", [1, hid], F32, kind="ExternalInput")
    su_d = nc.dram_tensor("su_row", [1, hid], F32, kind="ExternalInput")
    sd_d = nc.dram_tensor("sd_row", [1, D], F32, kind="ExternalInput")
    ident_d = nc.dram_tensor("ident", [128, 128], F32, kind="ExternalInput")
    ones_d = nc.dram_tensor("ones_in", [128, 128], F32, kind="ExternalInput")
    mask_d = nc.dram_tensor("mask", [128, 512], F32, kind="ExternalInput")

    out_d = nc.dram_tensor("out", [S, D], F32, kind="ExternalOutput")

    dbg = lambda want: "ExternalOutput" if (debug_stop == want) else "Internal"
    qt_s = nc.dram_tensor("qt_s", [D, S], BF16, kind=dbg("p1"))
    kt_s = nc.dram_tensor("kt_s", [D, S], BF16, kind=dbg("p1"))
    vt_s = nc.dram_tensor("vt_s", [D, S], BF16, kind=dbg("p1"))
    avt_dbg = nc.dram_tensor("avt_dbg", [D, S_half], BF16, kind="ExternalOutput") if debug_stop == "p2" else None
    z1t_dbg = nc.dram_tensor("z1t_dbg", [D, S_half], BF16, kind="ExternalOutput") if debug_stop == "p3" else None
    gu_s = nc.dram_tensor("gu_s", [S_half // 512, 128, nhc * 512], BF16, kind="Internal")
    wd_s = nc.dram_tensor("wd_s", [4, 128, nhc * 512], BF16, kind="Internal")
    r2_s = nc.dram_tensor("r2_s", [S_half, D], F32, kind=dbg("p4b"))

    ctx = ExitStack()
    with ctx:
        P = Prog(nc, ctx)
        sb = lambda nm, sh, dt: ctx.enter_context(nc.sbuf_tensor(nm, sh, dt))

        ident = sb("ident_sb", [128, 128], BF16)
        ones_col = sb("ones_col", [128, 1], BF16)
        ones_row = sb("ones_row", [1, 128], BF16)
        bo_bf = sb("bo_bf", [1, D], BF16)
        eps_col = sb("eps_col", [128, 1], F32)
        sbq = sb("sbq", [128, NCH], F32)
        sbk = sb("sbk", [128, NCH], F32)
        sbv = sb("sbv", [128, NCH], F32)
        mask_sb = sb("mask_sb", [128, 512], BF16)

        c0 = ExitStack()
        cst = c0.enter_context(nc.sbuf_tensor("cst", [128, 2048], F32))
        bo_row = c0.enter_context(nc.sbuf_tensor("bo_row_sb", [1, D], F32))

        def p0(P):
            P.dma("sync", cst[:, 0:128], ident_d[:], inc="c_ld")
            P.dma("sync", cst[:, 128:256], ones_d[:], inc="c_ld")
            P.dma("sync", bo_row[:], bo_d[:], inc="c_ld")
            P.dma("sync", sbq[:], bq_d[:], inc="c_ld")
            P.dma("sync", sbk[:], bk_d[:], inc="c_ld")
            P.dma("sync", sbv[:], bv_d[:], inc="c_ld")
            P.wait("vector", "c_ld")
            P.op("vector", lambda E: E.tensor_copy(ident[:], cst[:, 0:128]), inc="c_rdy")
            P.op("vector", lambda E: E.tensor_copy(ones_col[:], cst[:, 128:129]), inc="c_rdy")
            P.op("vector", lambda E: E.tensor_copy(ones_row[:], cst[0:1, 128:256]), inc="c_rdy")
            P.op("vector", lambda E: E.tensor_scalar(sbq[:], sbq[:], float(QK_SCALE), None, op0=ALU.mult), inc="c_rdy")
            P.op("vector", lambda E: E.tensor_scalar(sbk[:], sbk[:], float(QK_SCALE), None, op0=ALU.mult), inc="c_rdy")
            P.op("vector", lambda E: E.tensor_copy(bo_bf[:], bo_row[:]), inc="c_rdy")
            P.op("vector", lambda E: E.memset(eps_col[:], float(EPS)), inc="c_rdy")
            P.wait("sync", "c_rdy")
            P.dma("sync", cst[:, 0:512], mask_d[:], inc="c_ld")
            P.wait("vector", "c_ld")
            P.op("vector", lambda E: E.tensor_copy(mask_sb[:], cst[:, 0:512]), inc="c_rdy")

        P.phase("p0", p0)
        c0.close()
        NC_RDY = P.C["c_rdy"]

        with ExitStack() as c1:
            sb1 = lambda nm, sh, dt: c1.enter_context(nc.sbuf_tensor(nm, sh, dt))
            xth = sb1("xth", [128, NCH * S_half], BF16)
            w_half = [sb1(f"w_half{i}", [128, (NCH // 2) * D], BF16) for i in range(3)]
            o_stage = [sb1(f"o_stage{i}", [128, S_half], BF16) for i in range(2)]
            ps1 = [c1.enter_context(nc.psum_tensor(f"ps1_{i}", [128, 512], F32)) for i in range(2)]

            projs = [(kt_s, wkt_d, sbk, True), (vt_s, wvt_d, sbv, False), (qt_s, wqt_d, sbq, True)]

            def p1(P, hp):
                for cix in range(NCH):
                    P.dma("gpsimd", xth[:, cix * S_half:(cix + 1) * S_half],
                          xt_d[cix * 128:(cix + 1) * 128, hp * S_half:(hp + 1) * S_half],
                          inc="x_ld")
                xc = P.C["x_ld"]
                for pj, (scr, w_d, bias_t, scaled) in enumerate(projs):
                    gpj = P.C["p1_w5"]
                    for lo in range(2):
                        h5 = gpj + lo
                        need = ((h5 - 3) // 2 + 1) * NCH * nqt
                        P.wait("gpsimd", "p1_mm", min(need, P.C["p1_mm"]) if h5 >= 3 else 0)
                        for cq in range(NCH // 2):
                            cix = lo * (NCH // 2) + cq
                            P.dma("gpsimd", w_half[h5 % 3][:, cq * D:(cq + 1) * D],
                                  w_d[cix * 128:(cix + 1) * 128, :], inc=f"w_ld{h5 % 3}")
                        P.op("gpsimd", lambda E: None, inc="p1_w5")
                    W_RDY0 = P.C[f"w_ld{gpj % 3}"]
                    W_RDY1 = P.C[f"w_ld{(gpj + 1) % 3}"]
                    for ec in range(NCH):
                        for qt in range(nqt):
                            pp = (pj * NCH * nqt + ec * nqt + qt) % 2
                            if ec == 0 and qt == 0:
                                P.wait("tensor", f"w_ld{gpj % 3}", W_RDY0)
                                P.wait("tensor", f"w_ld{(gpj + 1) % 3}", W_RDY1)
                                if pj == 0:
                                    P.wait("tensor", "x_ld", xc)
                                    P.wait("tensor", "c_rdy", NC_RDY)
                                    P.wait("scalar", "c_rdy", NC_RDY)
                            P.wait("tensor", "p1_ev", P.C["p1_ev"] - 1)
                            for cix in range(NCH):
                                wb = w_half[(gpj + cix // (NCH // 2)) % 3]
                                cq = cix % (NCH // 2)
                                P.op("tensor", lambda E, cix=cix, cq=cq, wb=wb, ec=ec, pp=pp, qt=qt: E.matmul(
                                    ps1[pp][:],
                                    wb[:, cq * D + ec * 128: cq * D + ec * 128 + 128],
                                    xth[:, cix * S_half + qt * 512: cix * S_half + qt * 512 + 512],
                                    start=(cix == 0), stop=(cix == NCH - 1)),
                                    inc=("p1_mm" if cix == NCH - 1 else None))
                            b = (pj * NCH + ec) % 2
                            P.wait("scalar", "p1_mm")
                            if qt == 0:
                                P.wait("scalar", f"p1_st{b}")
                            P.op("scalar", lambda E, b=b, pp=pp, qt=qt, ec=ec, bias_t=bias_t, scaled=scaled:
                                 E.activation(o_stage[b][:, qt * 512:qt * 512 + 512], ps1[pp][:],
                                              AF.Identity,
                                              bias=bias_t[:, ec:ec + 1],
                                              scale=float(QK_SCALE) if scaled else 1.0),
                                 inc="p1_ev")
                        P.wait("sync", "p1_ev")
                        P.dma("sync", scr[ec * 128:(ec + 1) * 128, hp * S_half:(hp + 1) * S_half],
                              o_stage[b][:], inc=f"p1_st{b}")
                P.wait("sync", "p1_st0")
                P.wait("sync", "p1_st1")

            P.phase("p1_h0", lambda P: p1(P, 0))
            P.phase("p1_h1", lambda P: p1(P, 1))

        if debug_stop == "p1":
            return nc

        avt = sb("avt", [128, NCH * S_half], BF16)

        def make_p2(hp, c2):
            sb2 = lambda nm, sh, dt: c2.enter_context(nc.sbuf_tensor(nm + f"_h{hp}", sh, dt))
            SK = (hp + 1) * S_half
            nks = SK // 128
            QT2 = 256
            nqt2 = S_half // QT2
            nsub = S_half // 128
            k_sb = [sb2(f"k_sb{i}", [128, SK], BF16) for i in range(3)]
            vt_sb = [sb2(f"vt_sb{i}", [128, SK], BF16) for i in range(3)]
            q_sb = [sb2(f"q_sb{i}", [128, S_half], BF16) for i in range(3)]
            vaug = sb2("vaug", [128, nks * 130], BF16)
            expS = [sb2(f"expS{i}", [128, 1024], BF16) for i in range(3)]
            sqtmp = sb2("sqtmp", [128, S_half + SK], BF16)
            avtok_all = sb2("avtok_all", [128, nsub * 128], BF16)
            sums_all = sb2("sums_all", [128, nsub], F32)
            rec_all = sb2("rec_all", [128, nsub], F32)
            avtok_n = [sb2(f"avtok_n{i}", [128, 128], BF16) for i in range(2)]
            c_all = sb2("c_all", [128, H], F32)
            nrm = sb2("nrm", [1, 8], F32)
            nrm_bf = sb2("nrm_bf", [1, 1], BF16)
            rec = sb2("rec", [128, 2], F32)
            ps_s = [c2.enter_context(nc.psum_tensor(f"ps_s{i}_h{hp}", [128, 1024], F32)) for i in range(3)]
            ps_av = [c2.enter_context(nc.psum_tensor(f"ps_av{i}_h{hp}", [128, 512], F32)) for i in range(2)]

            def p2(P):
                P.op("vector", lambda E: E.memset(vaug[:], 1.0), inc="vaug_init")
                for h in range(H):
                    hb = h % 3
                    P.wait("sync", "head_done", P.C["head_done"] - 2)
                    P.wait("sync", "sq_k", P.C["sq_k"] - 2)
                    P.dma("sync", k_sb[hb][:], kt_s[h * 128:(h + 1) * 128, 0:SK], inc=f"kv_ld{hb}")
                    P.dma("sync", vt_sb[hb][:], vt_s[h * 128:(h + 1) * 128, 0:SK], inc=f"kv_ld{hb}")
                    P.dma("sync", q_sb[hb][:], qt_s[h * 128:(h + 1) * 128, hp * S_half:(hp + 1) * S_half], inc=f"kv_ld{hb}")
                    P.wait("tensor", f"kv_ld{hb}")

                    for g in range(nks // 4):
                        gb = g % 2
                        P.wait("tensor", "vtr_ev", P.C["vtr_ev"] - 1)
                        if g < 2:
                            P.wait("tensor", "avt_ev")
                        for j in range(4):
                            P.op("tensor", lambda E, g=g, j=j, gb=gb, hb=hb: E.matmul(
                                ps_av[gb][:, j * 128:(j + 1) * 128],
                                vt_sb[hb][:, (g * 4 + j) * 128:(g * 4 + j + 1) * 128],
                                ident[:], start=True, stop=True),
                                inc=("vtr_mm" if j == 3 else None))
                        P.wait("vector", "vtr_mm")
                        P.wait("vector", "vaug_init")
                        P.wait("vector", "avt_ev")
                        P.op("vector", lambda E, g=g, gb=gb: E.tensor_copy(
                            vaug[:, :].rearrange("p (k c) -> p k c", c=130)[:, g * 4:(g + 1) * 4, 0:128],
                            ps_av[gb][:, :].rearrange("p (k c) -> p k c", c=128)),
                            inc="vtr_ev")

                    P.wait("scalar", "nrm_mm")
                    P.wait("scalar", f"kv_ld{hb}")
                    P.op("scalar", lambda E, hb=hb: E.activation(sqtmp[:, 0:S_half], q_sb[hb][:], AF.Square),
                         inc="sq_q")
                    P.op("scalar", lambda E, hb=hb: E.activation(sqtmp[:, S_half:S_half + SK], k_sb[hb][:], AF.Square),
                         inc="sq_k")
                    P.wait("tensor", "vtr_ev")
                    P.wait("tensor", "vaug_init")
                    P.wait("tensor", "exp_done")
                    P.wait("tensor", "sq_q")
                    P.wait("tensor", "sq_k")
                    nt = (S_half + SK) // 512
                    for t in range(nt):
                        s6 = t % 6
                        P.wait("tensor", "nrm_red", P.C["nrm_red"] - 5)
                        P.op("tensor", lambda E, t=t, s6=s6: E.matmul(
                            ps_s[s6 // 2][0:1, (s6 % 2) * 512:(s6 % 2) * 512 + 512],
                            ones_col[:], sqtmp[:, t * 512:(t + 1) * 512],
                            start=True, stop=True), inc="nrm_mm")
                        P.wait("vector", "nrm_mm")
                        P.op("vector", lambda E, t=t, s6=s6: E.tensor_reduce(
                            nrm[0:1, 2 + (t % 2):3 + (t % 2)],
                            ps_s[s6 // 2][0:1, (s6 % 2) * 512:(s6 % 2) * 512 + 512], AX.X, ALU.max),
                            inc="nrm_red")
                        dst = 0 if t < S_half // 512 else 1
                        first = (t == 0) or (t == S_half // 512)
                        P.wait("vector", "nrm_red")
                        if first:
                            P.op("vector", lambda E, dst=dst, t=t: E.tensor_copy(
                                nrm[0:1, dst:dst + 1], nrm[0:1, 2 + (t % 2):3 + (t % 2)]), inc="nrm_acc")
                        else:
                            P.op("vector", lambda E, dst=dst, t=t: E.tensor_tensor(
                                nrm[0:1, dst:dst + 1], nrm[0:1, dst:dst + 1],
                                nrm[0:1, 2 + (t % 2):3 + (t % 2)], op=ALU.max), inc="nrm_acc")
                    P.wait("vector", "nrm_acc")
                    P.op("vector", lambda E: E.tensor_tensor(nrm[0:1, 4:5], nrm[0:1, 0:1], nrm[0:1, 1:2], op=ALU.mult),
                         inc="nrm_c")
                    P.wait("scalar", "nrm_c")
                    P.op("scalar", lambda E: E.activation(nrm[0:1, 5:6], nrm[0:1, 4:5], AF.Sqrt), inc="nrm_s")
                    P.wait("vector", "nrm_s")
                    P.op("vector", lambda E: E.tensor_scalar(nrm_bf[:], nrm[0:1, 5:6], -1.0, None, op0=ALU.mult),
                         inc="nrm_n")
                    P.wait("tensor", "nrm_n")
                    P.wait("tensor", "avt_ev")
                    P.wait("tensor", "vtr_ev")
                    P.op("tensor", lambda E: E.matmul(ps_av[0][:, 0:1], ones_row[:], nrm_bf[:],
                                                      start=True, stop=True), inc="nrm_b")
                    P.wait("vector", "exp_done")
                    P.wait("vector", "nrm_b")
                    P.op("vector", lambda E, h=h: E.tensor_copy(c_all[:, h:h + 1], ps_av[0][:, 0:1]), inc="c_rdy2")

                    glist = []
                    for qt in range(nqt2):
                        gi2 = hp * nqt2 + qt
                        nk = 2 * (gi2 + 1)
                        for s in range(0, nk, 4):
                            glist.append((qt, gi2, nk, list(range(s, min(s + 4, nk)))))

                    def emit_scores(i):
                        qt, gi2, nk, grp = glist[i]
                        par = P.C["sc_grp"] % 3
                        P.wait("tensor", "exp_done", P.C["sc_grp"] - 2)
                        if i == 0:
                            P.wait("tensor", "c_rdy2")
                            P.wait("scalar", "c_rdy2")
                        for jj, ks in enumerate(grp):
                            P.op("tensor", lambda E, jj=jj, ks=ks, par=par, hb=hb, qt=qt: E.matmul(
                                ps_s[par][:, jj * 256:(jj + 1) * 256],
                                k_sb[hb][:, ks * 128:(ks + 1) * 128],
                                q_sb[hb][:, qt * 256:(qt + 1) * 256],
                                start=True, stop=True),
                                inc=("sc_grp" if jj == len(grp) - 1 else None))

                    def emit_av(i):
                        qt, gi2, nk, grp = glist[i]
                        par = (P.C["av_grp"]) % 3
                        P.wait("scalar", "sc_grp", P.C["av_grp"] + 1)
                        P.wait("scalar", "av_grp", P.C["av_grp"] - 2)
                        P.op("scalar", lambda E, grp=grp, par=par, h=h: E.activation(
                            expS[par][:, 0:256 * len(grp)], ps_s[par][:, 0:256 * len(grp)],
                            AF.Exp, bias=c_all[:, h:h + 1], scale=1.0), inc="exp_done")
                        dmask = [(jj, ks) for jj, ks in enumerate(grp) if ks >= 2 * gi2]
                        if dmask:
                            P.wait("vector", "exp_done")
                            for jj, ks in dmask:
                                P.op("vector", lambda E, jj=jj, j=ks - 2 * gi2, par=par: E.tensor_tensor(
                                    expS[par][:, jj * 256:(jj + 1) * 256],
                                    expS[par][:, jj * 256:(jj + 1) * 256],
                                    mask_sb[:, j * 256:(j + 1) * 256], op=ALU.mult),
                                    inc=("mask_done" if (jj, ks) == dmask[-1] else None))
                        P.wait("tensor", "exp_done")
                        if dmask:
                            P.wait("tensor", "mask_done")
                        if grp[0] == 0:
                            P.wait("tensor", "av_ev")
                        for jj, ks in enumerate(grp):
                            for u in range(2):
                                P.op("tensor", lambda E, jj=jj, ks=ks, u=u, par=par, nk=nk: E.matmul(
                                    ps_av[u][:, 0:129],
                                    expS[par][:, jj * 256 + u * 128: jj * 256 + (u + 1) * 128],
                                    vaug[:, ks * 130: ks * 130 + 129],
                                    start=(ks == 0), stop=(ks == nk - 1)),
                                    inc=("av_grp" if (u == 1 and jj == len(grp) - 1) else None))
                        if ks == nk - 1:
                            P.wait("vector", "av_grp")
                            if qt == 0:
                                P.wait("vector", "avt_mm")
                            for u in range(2):
                                P.op("vector", lambda E, u=u: E.reciprocal(
                                    rec[:, u:u + 1], ps_av[u][:, 128:129]),
                                    inc=("rec_done" if u == 1 else None))
                            P.wait("vector", "rec_done")
                            for u in range(2):
                                P.op("vector", lambda E, u=u, qt=qt: E.tensor_scalar(
                                    avtok_all[:, (qt * 2 + u) * 128:(qt * 2 + u + 1) * 128],
                                    ps_av[u][:, 0:128],
                                    rec[:, u:u + 1], None, op0=ALU.mult),
                                    inc=("av_ev" if u == 1 else None))

                    emit_scores(0)
                    if len(glist) > 1:
                        emit_scores(1)
                    for i in range(len(glist)):
                        if i + 2 < len(glist):
                            emit_scores(i + 2)
                        emit_av(i)

                    P.wait("tensor", "av_ev")
                    for u in range(nsub):
                        P.wait("tensor", "avt_ev", P.C["avt_ev"] - 1)
                        P.op("tensor", lambda E, u=u: E.matmul(
                            ps_av[u % 2][:, 0:128],
                            avtok_all[:, u * 128:(u + 1) * 128], ident[:],
                            start=True, stop=True), inc="avt_mm")
                        P.wait("vector", "avt_mm")
                        P.op("vector", lambda E, u=u, h=h: E.tensor_copy(
                            avt[:, h * S_half + u * 128: h * S_half + (u + 1) * 128],
                            ps_av[u % 2][:, 0:128]),
                            inc=("avt_ev"))
                    P.op("vector", lambda E: E.memset(rec[:, 0:1], 0.0), inc="head_done")

            return p2

        def make_p3(hp, c3):
            sb3 = lambda nm, sh, dt: c3.enter_context(nc.sbuf_tensor(nm + f"_h{hp}", sh, dt))
            wo_bf = sb3("wo_bf", [128, NCH * D], BF16)
            x_tok = [sb3(f"x_tok{i}", [128, D], F32) for i in range(2)]
            r1 = sb3("r1", [128, D], F32)
            r1sq = sb3("r1sq", [128, D], F32)
            z1tok = [sb3(f"z1tok{i}", [128, D], BF16) for i in range(2)]
            st_ln = sb3("st_ln", [128, 8], F32)
            ps_o = c3.enter_context(nc.psum_tensor(f"ps_o_h{hp}", [128, 2048], F32))
            ps_z = c3.enter_context(nc.psum_tensor(f"ps_z_h{hp}", [128, 2048], F32))

            def p3(P):
                WO_RDY = {}
                for et in range(4):
                    for cix in range(NCH):
                        P.dma("gpsimd", wo_bf[:, cix * D + et * 512: cix * D + (et + 1) * 512],
                              wot_d[cix * 128:(cix + 1) * 128, et * 512:(et + 1) * 512], inc=f"wo_ld{et}")
                    WO_RDY[et] = P.C[f"wo_ld{et}"]

                def front(qc):
                    qg = hp * S_half + qc * 128
                    b = qc % 2
                    P.wait("sync", "r1_done", P.C["r1_done"] - 4 * 1)
                    P.dma("sync", x_tok[b][:], xtok_d[qg:qg + 128, :], inc=f"xr_ld{b}")
                    for et in range(4):
                        if qc == 0:
                            P.wait("tensor", f"wo_ld{et}", WO_RDY[et])
                        P.wait("tensor", "r1_done", P.C["r1_done"] - 2)
                        for cix in range(NCH):
                            P.op("tensor", lambda E, cix=cix, et=et, qc=qc: E.matmul(
                                ps_o[:, et * 512:(et + 1) * 512],
                                avt[:, cix * S_half + qc * 128: cix * S_half + (qc + 1) * 128],
                                wo_bf[:, cix * D + et * 512: cix * D + et * 512 + 512],
                                start=(cix == 0), stop=False),
                                inc=None)
                        P.op("tensor", lambda E, et=et: E.matmul(
                            ps_o[:, et * 512:(et + 1) * 512], ones_row[:], bo_bf[0:1, et * 512:(et + 1) * 512],
                            start=False, stop=True), inc="o_mm")
                        P.wait("vector", "o_mm")
                        P.wait("vector", f"xr_ld{b}")
                        if et == 0:
                            P.wait("vector", "r1sq_done")
                            P.wait("vector", "z1_done")
                        P.op("vector", lambda E, et=et, b=b: E.tensor_tensor(
                            r1[:, et * 512:(et + 1) * 512], ps_o[:, et * 512:(et + 1) * 512],
                            x_tok[b][:, et * 512:(et + 1) * 512], op=ALU.add), inc="r1_done")
                    P.wait("scalar", "r1_done")
                    P.wait("scalar", "ln_sum")
                    P.op("scalar", lambda E: E.activation(r1sq[:], r1[:], AF.Square), inc="r1sq_done")
                    P.wait("vector", "r1_done")
                    P.op("vector", lambda E: E.tensor_reduce(st_ln[:, 0:1], r1[:], AX.X, ALU.add), inc="ln_sum")
                    P.wait("vector", "r1sq_done")
                    P.op("vector", lambda E: E.tensor_reduce(st_ln[:, 1:2], r1sq[:], AX.X, ALU.add), inc="ln_sum")
                    P.wait("vector", "ln_sum")
                    P.op("vector", lambda E: E.tensor_scalar(st_ln[:, 2:3], st_ln[:, 0:1], 1.0 / D, None, op0=ALU.mult), inc="ln_mid")
                    P.wait("vector", "ln_mid")
                    P.op("vector", lambda E: E.tensor_tensor(st_ln[:, 3:4], st_ln[:, 2:3], st_ln[:, 2:3], op=ALU.mult), inc="ln_mid")
                    P.wait("vector", "ln_mid")
                    P.wait("vector", "ln_sqrt")
                    P.op("vector", lambda E: E.scalar_tensor_tensor(
                        st_ln[:, 4:5], st_ln[:, 1:2], 1.0 / D, st_ln[:, 3:4], op0=ALU.mult, op1=ALU.subtract),
                        inc="ln_var")
                    P.wait("scalar", "ln_var")
                    P.op("scalar", lambda E: E.activation(st_ln[:, 5:6], st_ln[:, 4:5], AF.Sqrt, bias=eps_col[:]),
                         inc="ln_sqrt")
                    P.wait("vector", "ln_sqrt")
                    P.op("vector", lambda E: E.reciprocal(st_ln[:, 6:7], st_ln[:, 5:6]), inc="ln_rstd")
                    P.wait("vector", "ln_rstd")
                    P.op("vector", lambda E: E.scalar_tensor_tensor(
                        st_ln[:, 7:8], st_ln[:, 2:3], -1.0, st_ln[:, 6:7], op0=ALU.mult, op1=ALU.mult),
                        inc="ln_nmr")
                    P.wait("vector", "ln_nmr")
                    P.wait("vector", "z1_mm")
                    P.op("vector", lambda E, b=b: E.tensor_scalar(
                        z1tok[b][:], r1[:], st_ln[:, 6:7], st_ln[:, 7:8], op0=ALU.mult, op1=ALU.add),
                        inc="z1_done")

                def back(qc):
                    b = qc % 2
                    P.wait("tensor", "z1_done", P.C["z1_done"] - (1 if qc == nqc - 1 else 2) + 1)
                    P.wait("tensor", "z1_ev")
                    for cix in range(NCH):
                        P.op("tensor", lambda E, cix=cix, b=b: E.matmul(
                            ps_z[:, cix * 128:(cix + 1) * 128], z1tok[b][:, cix * 128:(cix + 1) * 128],
                            ident[:], start=True, stop=True),
                            inc=("z1_mm" if cix == NCH - 1 else None))
                    P.wait("vector", "z1_mm")
                    for g in range(4):
                        P.op("vector", lambda E, g=g, qc=qc: E.tensor_copy(
                            avt[:, :].rearrange("p (c s) -> p c s", c=NCH)[:, 4 * g:4 * g + 4, qc * 128:(qc + 1) * 128],
                            ps_z[:, g * 512:(g + 1) * 512].rearrange("p (c s) -> p c s", c=4)),
                            inc=("z1_ev" if g == 3 else None))

                for qc in range(nqc + 1):
                    if qc < nqc:
                        front(qc)
                    if qc >= 1:
                        back(qc - 1)

            return p3

        def make_p4a(hp, c4):
            sb4 = lambda nm, sh, dt: c4.enter_context(nc.sbuf_tensor(nm + f"_h{hp}", sh, dt))
            sgb = sb4("sgb", [128, hid], BF16)
            sub_ = sb4("sub", [128, hid], BF16)
            sdb = sb4("sdb", [128, D], BF16)
            srow = sb4("srow", [1, 1024], F32)
            srow_bf = sb4("srow_bf", [1, 1024], BF16)
            wi_stage = [sb4(f"wi_stage{i}", [128, D], I32) for i in range(4)]
            wf_tmp = [sb4(f"wf_tmp{i}", [128, D], F32) for i in range(3)]
            w_g = [sb4(f"w_g{i}", [128, D], BF16) for i in range(3)]
            w_u = [sb4(f"w_u{i}", [128, D], BF16) for i in range(3)]
            wdi2 = [sb4(f"wdi2_{i}", [128, D], I32) for i in range(2)]
            wdc = [sb4(f"wdc{i}", [128, D], BF16) for i in range(1)]
            gate_t = sb4("gate_t", [128, 512], BF16)
            up_t = sb4("up_t", [128, 512], BF16)
            gu_t = [sb4(f"gu_t{i}", [128, 512], BF16) for i in range(2)]
            ps_g = [c4.enter_context(nc.psum_tensor(f"ps_g{i}_h{hp}", [128, 512], F32)) for i in range(3)]
            ps_u = [c4.enter_context(nc.psum_tensor(f"ps_u{i}_h{hp}", [128, 512], F32)) for i in range(3)]
            ps_b = c4.enter_context(nc.psum_tensor(f"ps_b_h{hp}", [128, 512], F32))

            def bcast_scales(P):
                for s_d_t, dst, n in [(sg_d, sgb, hid), (su_d, sub_, hid), (sd_d, sdb, D)]:
                    for off in range(0, n, 1024):
                        ln = min(1024, n - off)
                        P.wait("sync", "sb_mm", P.C["sb_mm"])
                        P.dma("sync", srow[0:1, 0:ln], s_d_t[0:1, off:off + ln], inc="sb_ld")
                        P.wait("vector", "sb_ld")
                        P.wait("vector", "sb_mm", P.C["sb_mm"])
                        P.op("vector", lambda E, ln=ln: E.tensor_copy(srow_bf[0:1, 0:ln], srow[0:1, 0:ln]), inc="sb_cast")
                        P.wait("tensor", "sb_cast")
                        for t in range(ln // 512):
                            P.wait("tensor", "sb_ev")
                            P.op("tensor", lambda E, t=t: E.matmul(
                                ps_b[:], ones_row[:], srow_bf[0:1, t * 512:(t + 1) * 512],
                                start=True, stop=True), inc="sb_mm")
                            P.wait("vector", "sb_mm")
                            P.op("vector", lambda E, t=t, off=off, dst=dst: E.tensor_copy(
                                dst[:, off + t * 512:off + (t + 1) * 512], ps_b[:]), inc="sb_ev")

            def p4a(P):
                bcast_scales(P)
                SB_EV = P.C["sb_ev"]
                WP_BASE = P.C["wp_sc"]

                def prep(hc):
                    for j3, (w_d, w_bf2, sb_t) in enumerate([(wgt_d, w_g, sgb), (wut_d, w_u, sub_)]):
                        it = hc * 3 + j3
                        wi = (hc * 2 + j3) % 4
                        wf = it % 3
                        b = hc % 3
                        P.wait("sync", "wp_shift", P.C["wp_shift"] - 3)
                        P.dma("sync", wi_stage[wi][:], w_d[hc, :, :], inc=f"wp_ld{wi}")
                        P.wait("scalar", f"wp_ld{wi}")
                        P.wait("scalar", "wp_sc", P.C["wp_sc"] - 2)
                        P.op("scalar", lambda E, wi=wi, wf=wf: E.activation(wf_tmp[wf][:], wi_stage[wi][:], AF.Copy, bias=-128.0),
                             inc="wp_shift")
                        P.wait("vector", "wp_shift")
                        if hc == 0 and j3 == 0:
                            P.wait("vector", "sb_ev", SB_EV)
                        P.wait("vector", "u_mm", P.C["u_mm"])
                        P.op("vector", lambda E, b=b, hc=hc, wf=wf, w_bf2=w_bf2, sb_t=sb_t: E.tensor_tensor(
                            w_bf2[b][:].rearrange("p (c e) -> p c e", c=NCH),
                            wf_tmp[wf][:].rearrange("p (c e) -> p c e", c=NCH),
                            sb_t[:, None, hc * 128:(hc + 1) * 128].broadcast_to([128, NCH, 128]),
                            op=ALU.mult), inc="wp_sc")
                    it = hc * 3 + 2
                    b2 = 0
                    wf = it % 3
                    bi = hc % 2
                    P.wait("sync", "wd_shift", P.C["wd_shift"] - 1)
                    P.dma("sync", wdi2[bi][:], wdt_d[hc * 128:(hc + 1) * 128, :], inc=f"wd_ld{bi}")
                    P.wait("scalar", f"wd_ld{bi}")
                    P.wait("scalar", "wp_sc", P.C["wp_sc"] - 2)
                    P.op("scalar", lambda E, bi=bi, wf=wf: E.activation(wf_tmp[wf][:], wdi2[bi][:], AF.Copy, bias=-128.0),
                         inc="wd_shift")
                    P.wait("vector", "wd_shift")
                    P.wait("vector", f"wd_st{b2}")
                    P.op("vector", lambda E, b2=b2, wf=wf: E.tensor_tensor(
                        wdc[b2][:], wf_tmp[wf][:], sdb[:], op=ALU.mult), inc="wp_sc")
                    P.wait("gpsimd", "wp_sc")
                    for et in range(4):
                        P.dma("gpsimd", wd_s[et, :, hc * 512:(hc + 1) * 512],
                              wdc[b2][:, et * 512:(et + 1) * 512], inc=f"wd_st{b2}")

                prep(0)
                prep(1)
                for hc in range(nhc):
                    b = hc % 3
                    if hc + 2 < nhc:
                        prep(hc + 2)
                    for qt in range(nqt):
                        pp = (hc * nqt + qt) % 3
                        P.wait("tensor", "wp_sc", WP_BASE + 3 * hc + 2)
                        P.wait("tensor", "g_act", P.C["g_act"] - 2)
                        P.wait("tensor", "u_cp", P.C["u_cp"] - 2)
                        for cix in range(NCH):
                            P.op("tensor", lambda E, cix=cix, pp=pp, qt=qt, b=b: E.matmul(
                                ps_g[pp][:],
                                w_g[b][:, cix * 128:(cix + 1) * 128],
                                avt[:, cix * S_half + qt * 512: cix * S_half + qt * 512 + 512],
                                start=(cix == 0), stop=(cix == NCH - 1)),
                                inc=("g_mm" if cix == NCH - 1 else None))
                        for cix in range(NCH):
                            P.op("tensor", lambda E, cix=cix, pp=pp, qt=qt, b=b: E.matmul(
                                ps_u[pp][:],
                                w_u[b][:, cix * 128:(cix + 1) * 128],
                                avt[:, cix * S_half + qt * 512: cix * S_half + qt * 512 + 512],
                                start=(cix == 0), stop=(cix == NCH - 1)),
                                inc=("u_mm" if cix == NCH - 1 else None))
                        P.wait("scalar", "g_mm")
                        P.wait("scalar", "gu_mul", P.C["gu_mul"])
                        P.op("scalar", lambda E, pp=pp: E.activation(gate_t[:], ps_g[pp][:], AF.Silu),
                             inc="g_act")
                        P.wait("vector", "u_mm")
                        P.wait("vector", "gu_mul", P.C["gu_mul"])
                        P.op("vector", lambda E, pp=pp: E.tensor_copy(up_t[:], ps_u[pp][:]), inc="u_cp")
                        gb = (hc * nqt + qt) % 2
                        P.wait("vector", "g_act")
                        P.wait("vector", "u_cp")
                        P.wait("vector", f"gu_st{gb}")
                        P.op("vector", lambda E, gb=gb: E.tensor_tensor(
                            gu_t[gb][:], gate_t[:], up_t[:], op=ALU.mult), inc="gu_mul")
                        P.wait("gpsimd", "gu_mul")
                        P.dma("gpsimd", gu_s[qt, :, hc * 512:(hc + 1) * 512], gu_t[gb][:], inc=f"gu_st{gb}")
                P.wait("gpsimd", "gu_st0")
                P.wait("gpsimd", "gu_st1")
                P.wait("gpsimd", "wd_st0")
                P.wait("gpsimd", "wd_st1")

            return p4a

        def make_p4b(hp, c4):
            sb4 = lambda nm, sh, dt: c4.enter_context(nc.sbuf_tensor(nm + f"_h{hp}", sh, dt))
            wd_bf = [sb4(f"wd_bf{i}", [128, (nhc // 2) * 512], BF16) for i in range(3)]
            gu_sl = [sb4(f"gu_sl{i}", [128, (nhc // 4) * 512], BF16) for i in range(3)]
            r2st = [sb4(f"r2st{i}", [128, 512], F32) for i in range(4)]
            ps_d = [c4.enter_context(nc.psum_tensor(f"ps_d{i}_h{hp}", [128, 512], F32)) for i in range(8)]
            nsl = nhc // 4
            nhf = nhc // 2

            def p4b(P):
                for et in range(4):
                    for half in range(2):
                        h5 = et * 2 + half
                        P.wait("sync", "d_half", P.C["d_half"] - (16 if half == 0 else 2))
                        P.dma("sync", wd_bf[h5 % 3][:],
                              wd_s[et, :, half * nhf * 512:(half + 1) * nhf * 512],
                              inc=f"wdb_ld{h5 % 3}")
                    for qt in range(nqt):
                        it3 = et * nqt + qt
                        dset = (it3 % 2) * 4
                        P.wait("tensor", "r2_ev", P.C["r2_ev"] - 1)
                        for u in range(4):
                            for g in range(4):
                                cix = et * 4 + g
                                P.op("tensor", lambda E, u=u, g=g, cix=cix, qt=qt, dset=dset: E.matmul(
                                    ps_d[dset + u][:, g * 128:(g + 1) * 128],
                                    avt[:, cix * S_half + qt * 512 + u * 128: cix * S_half + qt * 512 + (u + 1) * 128],
                                    ident[:], start=(g == 0), stop=False),
                                    inc=None)
                        for sl in range(4):
                            g3 = it3 * 4 + sl
                            h5 = et * 2 + sl // 2
                            P.wait("sync", "d_half", P.C["d_half"] - 2)
                            P.dma("sync", gu_sl[g3 % 3][:],
                                  gu_s[qt, :, sl * nsl * 512:(sl + 1) * nsl * 512],
                                  inc=f"gu_ld{g3 % 3}")
                            P.wait("tensor", f"gu_ld{g3 % 3}")
                            P.wait("tensor", f"wdb_ld{h5 % 3}")
                            for ci in range(nsl):
                                hc = sl * nsl + ci
                                for u in range(4):
                                    P.op("tensor", lambda E, ci=ci, u=u, hc=hc, g3=g3, h5=h5, dset=dset: E.matmul(
                                        ps_d[dset + u][:],
                                        gu_sl[g3 % 3][:, ci * 512 + u * 128: ci * 512 + (u + 1) * 128],
                                        wd_bf[h5 % 3][:, (hc - (h5 % 2) * nhf) * 512:(hc - (h5 % 2) * nhf + 1) * 512],
                                        start=False, stop=(hc == nhc - 1)),
                                        inc=("d_half" if (ci == nsl - 1 and u == 3) else None))
                        P.wait("vector", "d_half")
                        P.wait("vector", "r2_st")
                        for u in range(4):
                            P.op("vector", lambda E, u=u, dset=dset: E.tensor_copy(r2st[u][:], ps_d[dset + u][:]),
                                 inc=("r2_ev" if u == 3 else None))
                        P.wait("gpsimd", "r2_ev")
                        for u in range(4):
                            P.dma("gpsimd", r2_s[qt * 512 + u * 128: qt * 512 + (u + 1) * 128, et * 512:(et + 1) * 512],
                                  r2st[u][:], inc="r2_st")
                P.wait("gpsimd", "r2_st")

            return p4b

        def make_p4c(hp, c4):
            sb4 = lambda nm, sh, dt: c4.enter_context(nc.sbuf_tensor(nm + f"_h{hp}", sh, dt))
            r2 = [sb4(f"r2_{i}", [128, D], F32) for i in range(2)]
            r2sq = sb4("r2sq", [128, D], F32)
            o_t = [sb4(f"o_t{i}", [128, D], F32) for i in range(2)]
            st_ln = sb4("st_ln2", [128, 8], F32)

            def p4c(P):
                for qc in range(nqc):
                    b = qc % 2
                    P.wait("sync", "o_done", P.C["o_done"] - 1)
                    P.dma("sync", r2[b][:], r2_s[qc * 128:(qc + 1) * 128, :], inc=f"r2_ld{b}")
                    P.wait("scalar", f"r2_ld{b}")
                    P.wait("scalar", "ln2_sum")
                    P.op("scalar", lambda E, b=b: E.activation(r2sq[:], r2[b][:], AF.Square), inc="r2sq_done")
                    P.wait("vector", f"r2_ld{b}")
                    P.op("vector", lambda E, b=b: E.tensor_reduce(st_ln[:, 0:1], r2[b][:], AX.X, ALU.add),
                         inc="ln2_sum")
                    P.wait("vector", "r2sq_done")
                    P.op("vector", lambda E: E.tensor_reduce(st_ln[:, 1:2], r2sq[:], AX.X, ALU.add),
                         inc="ln2_sum")
                    P.wait("vector", "ln2_sum")
                    P.op("vector", lambda E: E.tensor_scalar(st_ln[:, 2:3], st_ln[:, 0:1], 1.0 / D, None, op0=ALU.mult), inc="ln2_mid")
                    P.wait("vector", "ln2_mid")
                    P.op("vector", lambda E: E.tensor_tensor(st_ln[:, 3:4], st_ln[:, 2:3], st_ln[:, 2:3], op=ALU.mult), inc="ln2_mid")
                    P.wait("vector", "ln2_mid")
                    P.wait("vector", "ln2_sqrt")
                    P.op("vector", lambda E: E.scalar_tensor_tensor(
                        st_ln[:, 4:5], st_ln[:, 1:2], 1.0 / D, st_ln[:, 3:4], op0=ALU.mult, op1=ALU.subtract),
                        inc="ln2_var")
                    P.wait("scalar", "ln2_var")
                    P.op("scalar", lambda E: E.activation(st_ln[:, 5:6], st_ln[:, 4:5], AF.Sqrt, bias=eps_col[:]),
                         inc="ln2_sqrt")
                    P.wait("vector", "ln2_sqrt")
                    P.op("vector", lambda E: E.reciprocal(st_ln[:, 6:7], st_ln[:, 5:6]), inc="ln2_rstd")
                    P.wait("vector", "ln2_rstd")
                    P.op("vector", lambda E: E.scalar_tensor_tensor(
                        st_ln[:, 7:8], st_ln[:, 2:3], -1.0, st_ln[:, 6:7], op0=ALU.mult, op1=ALU.mult),
                        inc="ln2_nmr")
                    P.wait("vector", "ln2_nmr")
                    P.wait("vector", f"o_st{b}")
                    P.op("vector", lambda E, b=b: E.tensor_scalar(
                        o_t[b][:], r2[b][:], st_ln[:, 6:7], st_ln[:, 7:8], op0=ALU.mult, op1=ALU.add),
                        inc="o_done")
                    P.wait("gpsimd", "o_done")
                    P.dma("gpsimd", out_d[hp * S_half + qc * 128: hp * S_half + (qc + 1) * 128, :],
                          o_t[b][:], inc=f"o_st{b}")
                P.wait("gpsimd", "o_st0")
                P.wait("gpsimd", "o_st1")

            return p4c

        for hp in range(2):
            with ExitStack() as c2:
                p2 = make_p2(hp, c2)
                P.phase(f"p2_h{hp}", p2)
            if debug_stop == "p2":
                with ExitStack() as cd:
                    st = cd.enter_context(nc.sbuf_tensor("avdbg_st", [128, S_half], BF16))

                    def dump(P):
                        for cix in range(NCH):
                            P.wait("vector", "d_st")
                            P.op("vector", lambda E, cix=cix: E.tensor_copy(
                                st[:], avt[:, cix * S_half:(cix + 1) * S_half]), inc="d_cp")
                            P.wait("gpsimd", "d_cp")
                            P.dma("gpsimd", avt_dbg[cix * 128:(cix + 1) * 128, :], st[:], inc="d_st")
                        P.wait("gpsimd", "d_st")
                    P.phase("dump", dump)
                return nc
            with ExitStack() as c3:
                p3 = make_p3(hp, c3)
                P.phase(f"p3_h{hp}", p3)
            if debug_stop == "p3":
                with ExitStack() as cd:
                    st = cd.enter_context(nc.sbuf_tensor("z1dbg_st", [128, S_half], BF16))

                    def dump(P):
                        for cix in range(NCH):
                            P.wait("vector", "d_st")
                            P.op("vector", lambda E, cix=cix: E.tensor_copy(
                                st[:], avt[:, cix * S_half:(cix + 1) * S_half]), inc="d_cp")
                            P.wait("gpsimd", "d_cp")
                            P.dma("gpsimd", z1t_dbg[cix * 128:(cix + 1) * 128, :], st[:], inc="d_st")
                        P.wait("gpsimd", "d_st")
                    P.phase("dump", dump)
                return nc
            with ExitStack() as c4:
                p4a = make_p4a(hp, c4)
                P.phase(f"p4a_h{hp}", p4a)
            with ExitStack() as c4:
                p4b = make_p4b(hp, c4)
                P.phase(f"p4b_h{hp}", p4b)
            if debug_stop == "p4b" and hp == 0:
                return nc
            with ExitStack() as c4:
                p4c = make_p4c(hp, c4)
                P.phase(f"p4c_h{hp}", p4c)

        return nc
    return nc


_T = lambda a: np.ascontiguousarray(np.asarray(a).T)


def host_layout(inputs, hid=HID):
    L = {}
    L["wqt"] = _T(inputs["L0_attn_Wq"])
    L["wkt"] = _T(inputs["L0_attn_Wk"])
    L["wvt"] = _T(inputs["L0_attn_Wv"])
    L["wot"] = _T(inputs["L0_attn_Wo"])
    L["bq"] = _T(np.asarray(inputs["L0_attn_bq"], dtype=np.float32).reshape(NCH, 128))
    L["bk"] = _T(np.asarray(inputs["L0_attn_bk"], dtype=np.float32).reshape(NCH, 128))
    L["bv"] = _T(np.asarray(inputs["L0_attn_bv"], dtype=np.float32).reshape(NCH, 128))
    L["bo_row"] = np.ascontiguousarray(np.asarray(inputs["L0_attn_bo"], dtype=np.float32).reshape(1, D))
    wg = np.asarray(inputs["L0_ffn_wq_gate"])
    wu = np.asarray(inputs["L0_ffn_wq_up"])
    L["wgt_r"] = np.ascontiguousarray(
        wg.reshape(hid // 128, 128, NCH, 128).transpose(0, 3, 2, 1).reshape(hid // 128, 128, NCH * 128))
    L["wut_r"] = np.ascontiguousarray(
        wu.reshape(hid // 128, 128, NCH, 128).transpose(0, 3, 2, 1).reshape(hid // 128, 128, NCH * 128))
    L["wdt"] = _T(inputs["L0_ffn_wq_down"])
    L["sg_row"] = np.ascontiguousarray(np.asarray(inputs["L0_ffn_s_gate"], dtype=np.float32).reshape(1, hid))
    L["su_row"] = np.ascontiguousarray(np.asarray(inputs["L0_ffn_s_up"], dtype=np.float32).reshape(1, hid))
    L["sd_row"] = np.ascontiguousarray(np.asarray(inputs["L0_ffn_s_down"], dtype=np.float32).reshape(1, D))
    L["ident"] = np.eye(128, dtype=np.float32)
    L["ones_in"] = np.ones((128, 128), dtype=np.float32)
    mask = np.zeros((128, 512), dtype=np.float32)
    qq = np.arange(256)
    for j in range(2):
        for kk in range(128):
            mask[kk, j * 256:(j + 1) * 256] = (qq >= j * 128 + kk).astype(np.float32)
    L["mask"] = mask
    return L


def kernel(**inputs):
    X = np.asarray(inputs["L0_attn_X"], dtype=np.float32)
    B, S, d = X.shape
    assert d == D
    L = host_layout(inputs)
    nc = build_program(S=S)
    in_maps = []
    for c in range(B):
        m = dict(L)
        m["xtok"] = np.ascontiguousarray(X[c])
        m["xt"] = _T(X[c])
        in_maps.append(m)
    res = run_bass_kernel_spmd(nc, in_maps, core_ids=list(range(B)))
    return np.stack([res.results[c]["out"] for c in range(B)], axis=0)


# version 14
# speedup vs baseline: 1.0514x; 1.0514x over previous
import numpy as np
from collections import defaultdict
import concourse.bass as bass
import concourse.mybir as mybir
from concourse.bass_utils import run_bass_kernel_spmd
from contextlib import ExitStack

F32 = mybir.dt.float32
BF16 = mybir.dt.bfloat16
I32 = mybir.dt.int32
AF = mybir.ActivationFunctionType
ALU = mybir.AluOpType
AX = mybir.AxisListType

ENGINES = ("sync", "tensor", "scalar", "vector", "gpsimd")

D = 2048
NCH = D // 128
H = 16
DH = 128
HID = 5632
QK_SCALE = 1.0 / np.sqrt(np.sqrt(DH))
EPS = 1e-5


class Prog:

    def __init__(self, nc, ctx):
        self.nc = nc
        self.ctx = ctx
        self.C = defaultdict(int)
        self.sems = {}
        self.cur = None
        self.E = None

    def sem(self, name):
        if name not in self.sems:
            self.sems[name] = self.ctx.enter_context(self.nc.semaphore(name))
        self.C.setdefault(name, 0)
        return self.sems[name]

    def phase(self, name, fn):
        snap = dict(self.C)
        final = None
        with self.nc.Block(name) as blk:
            for eng in ENGINES:
                self.C = defaultdict(int, snap)
                self.cur = eng

                def run(E, fn=fn):
                    self.E = E
                    fn(self)

                getattr(blk, eng)(run)
                if final is None:
                    final = dict(self.C)
                else:
                    assert final == dict(self.C), f"phase {name}: nondeterministic emission ({eng})"
        self.C = defaultdict(int, final)

    def wait(self, eng, sem, val=None):
        self.sem(sem)
        if val is None:
            val = self.C[sem]
        if val <= 0:
            return
        if self.cur == eng:
            self.E.wait_ge(self.sems[sem], val)

    def op(self, eng, emit, inc=None, inc_n=1):
        r = None
        if self.cur == eng:
            r = emit(self.E)
        if inc is not None:
            self.sem(inc)
            self.C[inc] += inc_n
            if r is not None:
                r.then_inc(self.sems[inc], inc_n)
        return r

    def dma(self, eng, dst, src, inc):
        self.op(eng, lambda E: E.dma_start(dst, src), inc=inc, inc_n=16)


def build_program(S=4096, hid=HID, debug_stop=None):
    nhc = hid // 128
    S_half = S // 2
    nqt = S_half // 512
    nqc = S_half // 128
    assert S % 1024 == 0

    nc = bass.Bass("TRN2", target_bir_lowering=False)

    xt_d = nc.dram_tensor("xt", [D, S], F32, kind="ExternalInput")
    xtok_d = nc.dram_tensor("xtok", [S, D], F32, kind="ExternalInput")
    wqt_d = nc.dram_tensor("wqt", [D, D], F32, kind="ExternalInput")
    wkt_d = nc.dram_tensor("wkt", [D, D], F32, kind="ExternalInput")
    wvt_d = nc.dram_tensor("wvt", [D, D], F32, kind="ExternalInput")
    wot_d = nc.dram_tensor("wot", [D, D], F32, kind="ExternalInput")
    bq_d = nc.dram_tensor("bq", [128, NCH], F32, kind="ExternalInput")
    bk_d = nc.dram_tensor("bk", [128, NCH], F32, kind="ExternalInput")
    bv_d = nc.dram_tensor("bv", [128, NCH], F32, kind="ExternalInput")
    bo_d = nc.dram_tensor("bo_row", [1, D], F32, kind="ExternalInput")
    wgt_d = nc.dram_tensor("wgt_r", [hid // 128, 128, D], I32, kind="ExternalInput")
    wut_d = nc.dram_tensor("wut_r", [hid // 128, 128, D], I32, kind="ExternalInput")
    wdt_d = nc.dram_tensor("wdt", [hid, D], I32, kind="ExternalInput")
    sg_d = nc.dram_tensor("sg_row", [1, hid], F32, kind="ExternalInput")
    su_d = nc.dram_tensor("su_row", [1, hid], F32, kind="ExternalInput")
    sd_d = nc.dram_tensor("sd_row", [1, D], F32, kind="ExternalInput")
    ident_d = nc.dram_tensor("ident", [128, 128], F32, kind="ExternalInput")
    ones_d = nc.dram_tensor("ones_in", [128, 128], F32, kind="ExternalInput")
    mask_d = nc.dram_tensor("mask", [128, 512], F32, kind="ExternalInput")

    out_d = nc.dram_tensor("out", [S, D], F32, kind="ExternalOutput")

    dbg = lambda want: "ExternalOutput" if (debug_stop == want) else "Internal"
    qt_s = nc.dram_tensor("qt_s", [D, S], BF16, kind=dbg("p1"))
    kt_s = nc.dram_tensor("kt_s", [D, S], BF16, kind=dbg("p1"))
    vt_s = nc.dram_tensor("vt_s", [D, S], BF16, kind=dbg("p1"))
    avt_dbg = nc.dram_tensor("avt_dbg", [D, S_half], BF16, kind="ExternalOutput") if debug_stop == "p2" else None
    z1t_dbg = nc.dram_tensor("z1t_dbg", [D, S_half], BF16, kind="ExternalOutput") if debug_stop == "p3" else None
    gu_s = nc.dram_tensor("gu_s", [S_half // 512, 128, nhc * 512], BF16, kind="Internal")
    wd_s = nc.dram_tensor("wd_s", [4, 128, nhc * 512], BF16, kind="Internal")
    r2_s = nc.dram_tensor("r2_s", [S_half, D], F32, kind=dbg("p4b"))

    ctx = ExitStack()
    with ctx:
        P = Prog(nc, ctx)
        sb = lambda nm, sh, dt: ctx.enter_context(nc.sbuf_tensor(nm, sh, dt))

        ident = sb("ident_sb", [128, 128], BF16)
        ones_col = sb("ones_col", [128, 1], BF16)
        ones_row = sb("ones_row", [1, 128], BF16)
        bo_bf = sb("bo_bf", [1, D], BF16)
        eps_col = sb("eps_col", [128, 1], F32)
        sbq = sb("sbq", [128, NCH], F32)
        sbk = sb("sbk", [128, NCH], F32)
        sbv = sb("sbv", [128, NCH], F32)
        mask_sb = sb("mask_sb", [128, 512], BF16)

        c0 = ExitStack()
        cst = c0.enter_context(nc.sbuf_tensor("cst", [128, 2048], F32))
        bo_row = c0.enter_context(nc.sbuf_tensor("bo_row_sb", [1, D], F32))

        def p0(P):
            P.dma("sync", cst[:, 0:128], ident_d[:], inc="c_ld")
            P.dma("sync", cst[:, 128:256], ones_d[:], inc="c_ld")
            P.dma("sync", bo_row[:], bo_d[:], inc="c_ld")
            P.dma("sync", sbq[:], bq_d[:], inc="c_ld")
            P.dma("sync", sbk[:], bk_d[:], inc="c_ld")
            P.dma("sync", sbv[:], bv_d[:], inc="c_ld")
            P.wait("vector", "c_ld")
            P.op("vector", lambda E: E.tensor_copy(ident[:], cst[:, 0:128]), inc="c_rdy")
            P.op("vector", lambda E: E.tensor_copy(ones_col[:], cst[:, 128:129]), inc="c_rdy")
            P.op("vector", lambda E: E.tensor_copy(ones_row[:], cst[0:1, 128:256]), inc="c_rdy")
            P.op("vector", lambda E: E.tensor_scalar(sbq[:], sbq[:], float(QK_SCALE), None, op0=ALU.mult), inc="c_rdy")
            P.op("vector", lambda E: E.tensor_scalar(sbk[:], sbk[:], float(QK_SCALE), None, op0=ALU.mult), inc="c_rdy")
            P.op("vector", lambda E: E.tensor_copy(bo_bf[:], bo_row[:]), inc="c_rdy")
            P.op("vector", lambda E: E.memset(eps_col[:], float(EPS)), inc="c_rdy")
            P.wait("sync", "c_rdy")
            P.dma("sync", cst[:, 0:512], mask_d[:], inc="c_ld")
            P.wait("vector", "c_ld")
            P.op("vector", lambda E: E.tensor_copy(mask_sb[:], cst[:, 0:512]), inc="c_rdy")

        P.phase("p0", p0)
        c0.close()
        NC_RDY = P.C["c_rdy"]

        with ExitStack() as c1:
            sb1 = lambda nm, sh, dt: c1.enter_context(nc.sbuf_tensor(nm, sh, dt))
            xth = sb1("xth", [128, NCH * S_half], BF16)
            w_half = [sb1(f"w_half{i}", [128, (NCH // 2) * D], BF16) for i in range(3)]
            o_stage = [sb1(f"o_stage{i}", [128, S_half], BF16) for i in range(2)]
            ps1 = [c1.enter_context(nc.psum_tensor(f"ps1_{i}", [128, 512], F32)) for i in range(2)]

            projs = [(kt_s, wkt_d, sbk, True), (vt_s, wvt_d, sbv, False), (qt_s, wqt_d, sbq, True)]

            def p1(P, hp):
                for cix in range(NCH):
                    P.dma("gpsimd", xth[:, cix * S_half:(cix + 1) * S_half],
                          xt_d[cix * 128:(cix + 1) * 128, hp * S_half:(hp + 1) * S_half],
                          inc="x_ld")
                xc = P.C["x_ld"]
                for pj, (scr, w_d, bias_t, scaled) in enumerate(projs):
                    gpj = P.C["p1_w5"]
                    for lo in range(2):
                        h5 = gpj + lo
                        need = ((h5 - 3) // 2 + 1) * NCH * nqt
                        P.wait("gpsimd", "p1_mm", min(need, P.C["p1_mm"]) if h5 >= 3 else 0)
                        for cq in range(NCH // 2):
                            cix = lo * (NCH // 2) + cq
                            P.dma("gpsimd", w_half[h5 % 3][:, cq * D:(cq + 1) * D],
                                  w_d[cix * 128:(cix + 1) * 128, :], inc=f"w_ld{h5 % 3}")
                        P.op("gpsimd", lambda E: None, inc="p1_w5")
                    W_RDY0 = P.C[f"w_ld{gpj % 3}"]
                    W_RDY1 = P.C[f"w_ld{(gpj + 1) % 3}"]
                    for ec in range(NCH):
                        for qt in range(nqt):
                            pp = (pj * NCH * nqt + ec * nqt + qt) % 2
                            if ec == 0 and qt == 0:
                                P.wait("tensor", f"w_ld{gpj % 3}", W_RDY0)
                                P.wait("tensor", f"w_ld{(gpj + 1) % 3}", W_RDY1)
                                if pj == 0:
                                    P.wait("tensor", "x_ld", xc)
                                    P.wait("tensor", "c_rdy", NC_RDY)
                                    P.wait("scalar", "c_rdy", NC_RDY)
                            P.wait("tensor", "p1_ev", P.C["p1_ev"] - 1)
                            for cix in range(NCH):
                                wb = w_half[(gpj + cix // (NCH // 2)) % 3]
                                cq = cix % (NCH // 2)
                                P.op("tensor", lambda E, cix=cix, cq=cq, wb=wb, ec=ec, pp=pp, qt=qt: E.matmul(
                                    ps1[pp][:],
                                    wb[:, cq * D + ec * 128: cq * D + ec * 128 + 128],
                                    xth[:, cix * S_half + qt * 512: cix * S_half + qt * 512 + 512],
                                    start=(cix == 0), stop=(cix == NCH - 1)),
                                    inc=("p1_mm" if cix == NCH - 1 else None))
                            b = (pj * NCH + ec) % 2
                            P.wait("scalar", "p1_mm")
                            if qt == 0:
                                P.wait("scalar", f"p1_st{b}")
                            P.op("scalar", lambda E, b=b, pp=pp, qt=qt, ec=ec, bias_t=bias_t, scaled=scaled:
                                 E.activation(o_stage[b][:, qt * 512:qt * 512 + 512], ps1[pp][:],
                                              AF.Identity,
                                              bias=bias_t[:, ec:ec + 1],
                                              scale=float(QK_SCALE) if scaled else 1.0),
                                 inc="p1_ev")
                        P.wait("sync", "p1_ev")
                        P.dma("sync", scr[ec * 128:(ec + 1) * 128, hp * S_half:(hp + 1) * S_half],
                              o_stage[b][:], inc=f"p1_st{b}")
                P.wait("sync", "p1_st0")
                P.wait("sync", "p1_st1")

            P.phase("p1_h0", lambda P: p1(P, 0))
            P.phase("p1_h1", lambda P: p1(P, 1))

        if debug_stop == "p1":
            return nc

        avt = sb("avt", [128, NCH * S_half], BF16)

        def make_p2(hp, c2):
            sb2 = lambda nm, sh, dt: c2.enter_context(nc.sbuf_tensor(nm + f"_h{hp}", sh, dt))
            SK = (hp + 1) * S_half
            nks = SK // 128
            QT2 = 256
            nqt2 = S_half // QT2
            nsub = S_half // 128
            k_sb = [sb2(f"k_sb{i}", [128, SK], BF16) for i in range(3)]
            vt_sb = [sb2(f"vt_sb{i}", [128, SK], BF16) for i in range(3)]
            q_sb = [sb2(f"q_sb{i}", [128, S_half], BF16) for i in range(3)]
            vaug = sb2("vaug", [128, nks * 130], BF16)
            expS = [sb2(f"expS{i}", [128, 1024], BF16) for i in range(3)]
            sqtmp = sb2("sqtmp", [128, S_half + SK], BF16)
            avtok_all = sb2("avtok_all", [128, nsub * 128], BF16)
            sums_all = sb2("sums_all", [128, nsub], F32)
            rec_all = sb2("rec_all", [128, nsub], F32)
            avtok_n = [sb2(f"avtok_n{i}", [128, 128], BF16) for i in range(2)]
            c_all = sb2("c_all", [128, H], F32)
            nrm = sb2("nrm", [1, 8], F32)
            nrm_bf = sb2("nrm_bf", [1, 1], BF16)
            rec = sb2("rec", [128, 2], F32)
            ps_s = [c2.enter_context(nc.psum_tensor(f"ps_s{i}_h{hp}", [128, 1024], F32)) for i in range(3)]
            ps_av = [c2.enter_context(nc.psum_tensor(f"ps_av{i}_h{hp}", [128, 512], F32)) for i in range(2)]

            def p2(P):
                P.op("vector", lambda E: E.memset(vaug[:], 1.0), inc="vaug_init")
                for h in range(H):
                    hb = h % 3
                    P.wait("sync", "head_done", P.C["head_done"] - 2)
                    P.wait("sync", "sq_k", P.C["sq_k"] - 2)
                    P.dma("sync", k_sb[hb][:], kt_s[h * 128:(h + 1) * 128, 0:SK], inc=f"kv_ld{hb}")
                    P.dma("sync", vt_sb[hb][:], vt_s[h * 128:(h + 1) * 128, 0:SK], inc=f"kv_ld{hb}")
                    P.dma("sync", q_sb[hb][:], qt_s[h * 128:(h + 1) * 128, hp * S_half:(hp + 1) * S_half], inc=f"kv_ld{hb}")
                    P.wait("tensor", f"kv_ld{hb}")

                    for g in range(nks // 4):
                        gb = g % 2
                        P.wait("tensor", "vtr_ev", P.C["vtr_ev"] - 1)
                        if g < 2:
                            P.wait("tensor", "avt_ev")
                        for j in range(4):
                            P.op("tensor", lambda E, g=g, j=j, gb=gb, hb=hb: E.matmul(
                                ps_av[gb][:, j * 128:(j + 1) * 128],
                                vt_sb[hb][:, (g * 4 + j) * 128:(g * 4 + j + 1) * 128],
                                ident[:], start=True, stop=True),
                                inc=("vtr_mm" if j == 3 else None))
                        P.wait("vector", "vtr_mm")
                        P.wait("vector", "vaug_init")
                        P.wait("vector", "avt_ev")
                        P.op("vector", lambda E, g=g, gb=gb: E.tensor_copy(
                            vaug[:, :].rearrange("p (k c) -> p k c", c=130)[:, g * 4:(g + 1) * 4, 0:128],
                            ps_av[gb][:, :].rearrange("p (k c) -> p k c", c=128)),
                            inc="vtr_ev")

                    P.wait("scalar", "nrm_mm")
                    P.wait("scalar", f"kv_ld{hb}")
                    P.op("scalar", lambda E, hb=hb: E.activation(sqtmp[:, 0:S_half], q_sb[hb][:], AF.Square),
                         inc="sq_q")
                    P.op("scalar", lambda E, hb=hb: E.activation(sqtmp[:, S_half:S_half + SK], k_sb[hb][:], AF.Square),
                         inc="sq_k")
                    P.wait("tensor", "vtr_ev")
                    P.wait("tensor", "vaug_init")
                    P.wait("tensor", "exp_done")
                    P.wait("tensor", "sq_q")
                    P.wait("tensor", "sq_k")
                    nt = (S_half + SK) // 512
                    for t in range(nt):
                        s6 = t % 6
                        P.wait("tensor", "nrm_red", P.C["nrm_red"] - 5)
                        P.op("tensor", lambda E, t=t, s6=s6: E.matmul(
                            ps_s[s6 // 2][0:1, (s6 % 2) * 512:(s6 % 2) * 512 + 512],
                            ones_col[:], sqtmp[:, t * 512:(t + 1) * 512],
                            start=True, stop=True), inc="nrm_mm")
                        P.wait("vector", "nrm_mm")
                        P.op("vector", lambda E, t=t, s6=s6: E.tensor_reduce(
                            nrm[0:1, 2 + (t % 2):3 + (t % 2)],
                            ps_s[s6 // 2][0:1, (s6 % 2) * 512:(s6 % 2) * 512 + 512], AX.X, ALU.max),
                            inc="nrm_red")
                        dst = 0 if t < S_half // 512 else 1
                        first = (t == 0) or (t == S_half // 512)
                        P.wait("vector", "nrm_red")
                        if first:
                            P.op("vector", lambda E, dst=dst, t=t: E.tensor_copy(
                                nrm[0:1, dst:dst + 1], nrm[0:1, 2 + (t % 2):3 + (t % 2)]), inc="nrm_acc")
                        else:
                            P.op("vector", lambda E, dst=dst, t=t: E.tensor_tensor(
                                nrm[0:1, dst:dst + 1], nrm[0:1, dst:dst + 1],
                                nrm[0:1, 2 + (t % 2):3 + (t % 2)], op=ALU.max), inc="nrm_acc")
                    P.wait("vector", "nrm_acc")
                    P.op("vector", lambda E: E.tensor_tensor(nrm[0:1, 4:5], nrm[0:1, 0:1], nrm[0:1, 1:2], op=ALU.mult),
                         inc="nrm_c")
                    P.wait("scalar", "nrm_c")
                    P.op("scalar", lambda E: E.activation(nrm[0:1, 5:6], nrm[0:1, 4:5], AF.Sqrt), inc="nrm_s")
                    P.wait("vector", "nrm_s")
                    P.op("vector", lambda E: E.tensor_scalar(nrm_bf[:], nrm[0:1, 5:6], -1.0, None, op0=ALU.mult),
                         inc="nrm_n")
                    P.wait("tensor", "nrm_n")
                    P.wait("tensor", "avt_ev")
                    P.wait("tensor", "vtr_ev")
                    P.op("tensor", lambda E: E.matmul(ps_av[0][:, 0:1], ones_row[:], nrm_bf[:],
                                                      start=True, stop=True), inc="nrm_b")
                    P.wait("vector", "exp_done")
                    P.wait("vector", "nrm_b")
                    P.op("vector", lambda E, h=h: E.tensor_copy(c_all[:, h:h + 1], ps_av[0][:, 0:1]), inc="c_rdy2")

                    glist = []
                    for qt in range(nqt2):
                        gi2 = hp * nqt2 + qt
                        nk = 2 * (gi2 + 1)
                        for s in range(0, nk, 4):
                            glist.append((qt, gi2, nk, list(range(s, min(s + 4, nk)))))

                    def emit_scores(i):
                        qt, gi2, nk, grp = glist[i]
                        par = P.C["sc_grp"] % 3
                        P.wait("tensor", "exp_done", P.C["sc_grp"] - 2)
                        if i == 0:
                            P.wait("tensor", "c_rdy2")
                            P.wait("scalar", "c_rdy2")
                        for jj, ks in enumerate(grp):
                            P.op("tensor", lambda E, jj=jj, ks=ks, par=par, hb=hb, qt=qt: E.matmul(
                                ps_s[par][:, jj * 256:(jj + 1) * 256],
                                k_sb[hb][:, ks * 128:(ks + 1) * 128],
                                q_sb[hb][:, qt * 256:(qt + 1) * 256],
                                start=True, stop=True),
                                inc=("sc_grp" if jj == len(grp) - 1 else None))

                    def emit_av(i):
                        qt, gi2, nk, grp = glist[i]
                        par = (P.C["av_grp"]) % 3
                        P.wait("scalar", "sc_grp", P.C["av_grp"] + 1)
                        P.wait("scalar", "av_grp", P.C["av_grp"] - 2)
                        P.op("scalar", lambda E, grp=grp, par=par, h=h: E.activation(
                            expS[par][:, 0:256 * len(grp)], ps_s[par][:, 0:256 * len(grp)],
                            AF.Exp, bias=c_all[:, h:h + 1], scale=1.0), inc="exp_done")
                        dmask = [(jj, ks) for jj, ks in enumerate(grp) if ks >= 2 * gi2]
                        if dmask:
                            P.wait("vector", "exp_done")
                            for jj, ks in dmask:
                                P.op("vector", lambda E, jj=jj, j=ks - 2 * gi2, par=par: E.tensor_tensor(
                                    expS[par][:, jj * 256:(jj + 1) * 256],
                                    expS[par][:, jj * 256:(jj + 1) * 256],
                                    mask_sb[:, j * 256:(j + 1) * 256], op=ALU.mult),
                                    inc=("mask_done" if (jj, ks) == dmask[-1] else None))
                        P.wait("tensor", "exp_done")
                        if dmask:
                            P.wait("tensor", "mask_done")
                        if grp[0] == 0:
                            P.wait("tensor", "av_ev")
                        for jj, ks in enumerate(grp):
                            for u in range(2):
                                P.op("tensor", lambda E, jj=jj, ks=ks, u=u, par=par, nk=nk: E.matmul(
                                    ps_av[u][:, 0:129],
                                    expS[par][:, jj * 256 + u * 128: jj * 256 + (u + 1) * 128],
                                    vaug[:, ks * 130: ks * 130 + 129],
                                    start=(ks == 0), stop=(ks == nk - 1)),
                                    inc=("av_grp" if (u == 1 and jj == len(grp) - 1) else None))
                        if ks == nk - 1:
                            P.wait("vector", "av_grp")
                            if qt == 0:
                                P.wait("vector", "avt_mm")
                            for u in range(2):
                                P.op("vector", lambda E, u=u: E.reciprocal(
                                    rec[:, u:u + 1], ps_av[u][:, 128:129]),
                                    inc=("rec_done" if u == 1 else None))
                            P.wait("vector", "rec_done")
                            for u in range(2):
                                P.op("vector", lambda E, u=u, qt=qt: E.tensor_scalar(
                                    avtok_all[:, (qt * 2 + u) * 128:(qt * 2 + u + 1) * 128],
                                    ps_av[u][:, 0:128],
                                    rec[:, u:u + 1], None, op0=ALU.mult),
                                    inc=("av_ev" if u == 1 else None))

                    emit_scores(0)
                    if len(glist) > 1:
                        emit_scores(1)
                    for i in range(len(glist)):
                        if i + 2 < len(glist):
                            emit_scores(i + 2)
                        emit_av(i)

                    P.wait("tensor", "av_ev")
                    for u in range(nsub):
                        P.wait("tensor", "avt_ev", P.C["avt_ev"] - 1)
                        P.op("tensor", lambda E, u=u: E.matmul(
                            ps_av[u % 2][:, 0:128],
                            avtok_all[:, u * 128:(u + 1) * 128], ident[:],
                            start=True, stop=True), inc="avt_mm")
                        P.wait("vector", "avt_mm")
                        P.op("vector", lambda E, u=u, h=h: E.tensor_copy(
                            avt[:, h * S_half + u * 128: h * S_half + (u + 1) * 128],
                            ps_av[u % 2][:, 0:128]),
                            inc=("avt_ev"))
                    P.op("vector", lambda E: E.memset(rec[:, 0:1], 0.0), inc="head_done")

            return p2

        def make_p3(hp, c3):
            sb3 = lambda nm, sh, dt: c3.enter_context(nc.sbuf_tensor(nm + f"_h{hp}", sh, dt))
            wo_bf = sb3("wo_bf", [128, NCH * D], BF16)
            x_tok = [sb3(f"x_tok{i}", [128, D], F32) for i in range(2)]
            r1 = sb3("r1", [128, D], F32)
            r1sq = sb3("r1sq", [128, D], F32)
            z1tok = [sb3(f"z1tok{i}", [128, D], BF16) for i in range(2)]
            st_ln = sb3("st_ln", [128, 8], F32)
            ps_o = c3.enter_context(nc.psum_tensor(f"ps_o_h{hp}", [128, 2048], F32))
            ps_z = c3.enter_context(nc.psum_tensor(f"ps_z_h{hp}", [128, 2048], F32))

            def p3(P):
                WO_RDY = {}
                for et in range(4):
                    for cix in range(NCH):
                        P.dma("gpsimd", wo_bf[:, cix * D + et * 512: cix * D + (et + 1) * 512],
                              wot_d[cix * 128:(cix + 1) * 128, et * 512:(et + 1) * 512], inc=f"wo_ld{et}")
                    WO_RDY[et] = P.C[f"wo_ld{et}"]

                def front(qc):
                    qg = hp * S_half + qc * 128
                    b = qc % 2
                    P.wait("sync", "r1_done", P.C["r1_done"] - 4 * 1)
                    P.dma("sync", x_tok[b][:], xtok_d[qg:qg + 128, :], inc=f"xr_ld{b}")
                    for et in range(4):
                        if qc == 0:
                            P.wait("tensor", f"wo_ld{et}", WO_RDY[et])
                        P.wait("tensor", "r1_done", P.C["r1_done"] - 2)
                        for cix in range(NCH):
                            P.op("tensor", lambda E, cix=cix, et=et, qc=qc: E.matmul(
                                ps_o[:, et * 512:(et + 1) * 512],
                                avt[:, cix * S_half + qc * 128: cix * S_half + (qc + 1) * 128],
                                wo_bf[:, cix * D + et * 512: cix * D + et * 512 + 512],
                                start=(cix == 0), stop=False),
                                inc=None)
                        P.op("tensor", lambda E, et=et: E.matmul(
                            ps_o[:, et * 512:(et + 1) * 512], ones_row[:], bo_bf[0:1, et * 512:(et + 1) * 512],
                            start=False, stop=True), inc="o_mm")
                        P.wait("vector", "o_mm")
                        P.wait("vector", f"xr_ld{b}")
                        if et == 0:
                            P.wait("vector", "r1sq_done")
                            P.wait("vector", "z1_done")
                        P.op("vector", lambda E, et=et, b=b: E.tensor_tensor(
                            r1[:, et * 512:(et + 1) * 512], ps_o[:, et * 512:(et + 1) * 512],
                            x_tok[b][:, et * 512:(et + 1) * 512], op=ALU.add), inc="r1_done")
                    P.wait("scalar", "r1_done")
                    P.wait("scalar", "ln_mid")
                    P.op("scalar", lambda E: E.activation(r1sq[:], r1[:], AF.Copy, accum_out=st_ln[:, 0:1]),
                         inc="r1sq_done")
                    P.wait("scalar", "r1sq_done")
                    P.op("scalar", lambda E: E.activation(r1sq[:], r1[:], AF.Square, accum_out=st_ln[:, 1:2]),
                         inc="ln_sum")
                    P.wait("vector", "ln_sum")
                    P.op("vector", lambda E: E.tensor_scalar(st_ln[:, 2:3], st_ln[:, 0:1], 1.0 / D, None, op0=ALU.mult), inc="ln_mid")
                    P.wait("vector", "ln_mid")
                    P.op("vector", lambda E: E.tensor_tensor(st_ln[:, 3:4], st_ln[:, 2:3], st_ln[:, 2:3], op=ALU.mult), inc="ln_mid")
                    P.wait("vector", "ln_mid")
                    P.wait("vector", "ln_sqrt")
                    P.op("vector", lambda E: E.scalar_tensor_tensor(
                        st_ln[:, 4:5], st_ln[:, 1:2], 1.0 / D, st_ln[:, 3:4], op0=ALU.mult, op1=ALU.subtract),
                        inc="ln_var")
                    P.wait("scalar", "ln_var")
                    P.op("scalar", lambda E: E.activation(st_ln[:, 5:6], st_ln[:, 4:5], AF.Sqrt, bias=eps_col[:]),
                         inc="ln_sqrt")
                    P.wait("vector", "ln_sqrt")
                    P.op("vector", lambda E: E.reciprocal(st_ln[:, 6:7], st_ln[:, 5:6]), inc="ln_rstd")
                    P.wait("vector", "ln_rstd")
                    P.op("vector", lambda E: E.scalar_tensor_tensor(
                        st_ln[:, 7:8], st_ln[:, 2:3], -1.0, st_ln[:, 6:7], op0=ALU.mult, op1=ALU.mult),
                        inc="ln_nmr")
                    P.wait("vector", "ln_nmr")
                    P.wait("vector", "z1_mm")
                    P.op("vector", lambda E, b=b: E.tensor_scalar(
                        z1tok[b][:], r1[:], st_ln[:, 6:7], st_ln[:, 7:8], op0=ALU.mult, op1=ALU.add),
                        inc="z1_done")

                def back(qc):
                    b = qc % 2
                    P.wait("tensor", "z1_done", P.C["z1_done"] - (1 if qc == nqc - 1 else 2) + 1)
                    P.wait("tensor", "z1_ev")
                    for cix in range(NCH):
                        P.op("tensor", lambda E, cix=cix, b=b: E.matmul(
                            ps_z[:, cix * 128:(cix + 1) * 128], z1tok[b][:, cix * 128:(cix + 1) * 128],
                            ident[:], start=True, stop=True),
                            inc=("z1_mm" if cix == NCH - 1 else None))
                    P.wait("vector", "z1_mm")
                    for g in range(4):
                        P.op("vector", lambda E, g=g, qc=qc: E.tensor_copy(
                            avt[:, :].rearrange("p (c s) -> p c s", c=NCH)[:, 4 * g:4 * g + 4, qc * 128:(qc + 1) * 128],
                            ps_z[:, g * 512:(g + 1) * 512].rearrange("p (c s) -> p c s", c=4)),
                            inc=("z1_ev" if g == 3 else None))

                for qc in range(nqc + 1):
                    if qc < nqc:
                        front(qc)
                    if qc >= 1:
                        back(qc - 1)

            return p3

        def make_p4a(hp, c4):
            sb4 = lambda nm, sh, dt: c4.enter_context(nc.sbuf_tensor(nm + f"_h{hp}", sh, dt))
            sgb = sb4("sgb", [128, hid], BF16)
            sub_ = sb4("sub", [128, hid], BF16)
            sdb = sb4("sdb", [128, D], BF16)
            srow = sb4("srow", [1, 1024], F32)
            srow_bf = sb4("srow_bf", [1, 1024], BF16)
            wi_stage = [sb4(f"wi_stage{i}", [128, D], I32) for i in range(4)]
            wf_tmp = [sb4(f"wf_tmp{i}", [128, D], F32) for i in range(3)]
            w_g = [sb4(f"w_g{i}", [128, D], BF16) for i in range(3)]
            w_u = [sb4(f"w_u{i}", [128, D], BF16) for i in range(3)]
            wdi2 = [sb4(f"wdi2_{i}", [128, D], I32) for i in range(2)]
            wdc = [sb4(f"wdc{i}", [128, D], BF16) for i in range(1)]
            gate_t = sb4("gate_t", [128, 512], BF16)
            up_t = sb4("up_t", [128, 512], BF16)
            gu_t = [sb4(f"gu_t{i}", [128, 512], BF16) for i in range(2)]
            ps_g = [c4.enter_context(nc.psum_tensor(f"ps_g{i}_h{hp}", [128, 512], F32)) for i in range(3)]
            ps_u = [c4.enter_context(nc.psum_tensor(f"ps_u{i}_h{hp}", [128, 512], F32)) for i in range(3)]
            ps_b = c4.enter_context(nc.psum_tensor(f"ps_b_h{hp}", [128, 512], F32))

            def bcast_scales(P):
                for s_d_t, dst, n in [(sg_d, sgb, hid), (su_d, sub_, hid), (sd_d, sdb, D)]:
                    for off in range(0, n, 1024):
                        ln = min(1024, n - off)
                        P.wait("sync", "sb_mm", P.C["sb_mm"])
                        P.dma("sync", srow[0:1, 0:ln], s_d_t[0:1, off:off + ln], inc="sb_ld")
                        P.wait("vector", "sb_ld")
                        P.wait("vector", "sb_mm", P.C["sb_mm"])
                        P.op("vector", lambda E, ln=ln: E.tensor_copy(srow_bf[0:1, 0:ln], srow[0:1, 0:ln]), inc="sb_cast")
                        P.wait("tensor", "sb_cast")
                        for t in range(ln // 512):
                            P.wait("tensor", "sb_ev")
                            P.op("tensor", lambda E, t=t: E.matmul(
                                ps_b[:], ones_row[:], srow_bf[0:1, t * 512:(t + 1) * 512],
                                start=True, stop=True), inc="sb_mm")
                            P.wait("vector", "sb_mm")
                            P.op("vector", lambda E, t=t, off=off, dst=dst: E.tensor_copy(
                                dst[:, off + t * 512:off + (t + 1) * 512], ps_b[:]), inc="sb_ev")

            def p4a(P):
                bcast_scales(P)
                SB_EV = P.C["sb_ev"]
                WP_BASE = P.C["wp_sc"]

                def prep(hc):
                    for j3, (w_d, w_bf2, sb_t) in enumerate([(wgt_d, w_g, sgb), (wut_d, w_u, sub_)]):
                        it = hc * 3 + j3
                        wi = (hc * 2 + j3) % 4
                        wf = it % 3
                        b = hc % 3
                        P.wait("sync", "wp_shift", P.C["wp_shift"] - 3)
                        P.dma("sync", wi_stage[wi][:], w_d[hc, :, :], inc=f"wp_ld{wi}")
                        P.wait("scalar", f"wp_ld{wi}")
                        P.wait("scalar", "wp_sc", P.C["wp_sc"] - 2)
                        P.op("scalar", lambda E, wi=wi, wf=wf: E.activation(wf_tmp[wf][:], wi_stage[wi][:], AF.Copy, bias=-128.0),
                             inc="wp_shift")
                        P.wait("vector", "wp_shift")
                        if hc == 0 and j3 == 0:
                            P.wait("vector", "sb_ev", SB_EV)
                        P.wait("vector", "u_mm", P.C["u_mm"])
                        P.op("vector", lambda E, b=b, hc=hc, wf=wf, w_bf2=w_bf2, sb_t=sb_t: E.tensor_tensor(
                            w_bf2[b][:].rearrange("p (c e) -> p c e", c=NCH),
                            wf_tmp[wf][:].rearrange("p (c e) -> p c e", c=NCH),
                            sb_t[:, None, hc * 128:(hc + 1) * 128].broadcast_to([128, NCH, 128]),
                            op=ALU.mult), inc="wp_sc")
                    it = hc * 3 + 2
                    b2 = 0
                    wf = it % 3
                    bi = hc % 2
                    P.wait("sync", "wd_shift", P.C["wd_shift"] - 1)
                    P.dma("sync", wdi2[bi][:], wdt_d[hc * 128:(hc + 1) * 128, :], inc=f"wd_ld{bi}")
                    P.wait("scalar", f"wd_ld{bi}")
                    P.wait("scalar", "wp_sc", P.C["wp_sc"] - 2)
                    P.op("scalar", lambda E, bi=bi, wf=wf: E.activation(wf_tmp[wf][:], wdi2[bi][:], AF.Copy, bias=-128.0),
                         inc="wd_shift")
                    P.wait("vector", "wd_shift")
                    P.wait("vector", f"wd_st{b2}")
                    P.op("vector", lambda E, b2=b2, wf=wf: E.tensor_tensor(
                        wdc[b2][:], wf_tmp[wf][:], sdb[:], op=ALU.mult), inc="wp_sc")
                    P.wait("gpsimd", "wp_sc")
                    for et in range(4):
                        P.dma("gpsimd", wd_s[et, :, hc * 512:(hc + 1) * 512],
                              wdc[b2][:, et * 512:(et + 1) * 512], inc=f"wd_st{b2}")

                prep(0)
                prep(1)
                for hc in range(nhc):
                    b = hc % 3
                    if hc + 2 < nhc:
                        prep(hc + 2)
                    for qt in range(nqt):
                        pp = (hc * nqt + qt) % 3
                        P.wait("tensor", "wp_sc", WP_BASE + 3 * hc + 2)
                        P.wait("tensor", "g_act", P.C["g_act"] - 2)
                        P.wait("tensor", "u_cp", P.C["u_cp"] - 2)
                        for cix in range(NCH):
                            P.op("tensor", lambda E, cix=cix, pp=pp, qt=qt, b=b: E.matmul(
                                ps_g[pp][:],
                                w_g[b][:, cix * 128:(cix + 1) * 128],
                                avt[:, cix * S_half + qt * 512: cix * S_half + qt * 512 + 512],
                                start=(cix == 0), stop=(cix == NCH - 1)),
                                inc=("g_mm" if cix == NCH - 1 else None))
                        for cix in range(NCH):
                            P.op("tensor", lambda E, cix=cix, pp=pp, qt=qt, b=b: E.matmul(
                                ps_u[pp][:],
                                w_u[b][:, cix * 128:(cix + 1) * 128],
                                avt[:, cix * S_half + qt * 512: cix * S_half + qt * 512 + 512],
                                start=(cix == 0), stop=(cix == NCH - 1)),
                                inc=("u_mm" if cix == NCH - 1 else None))
                        P.wait("scalar", "g_mm")
                        P.wait("scalar", "gu_mul", P.C["gu_mul"])
                        P.op("scalar", lambda E, pp=pp: E.activation(gate_t[:], ps_g[pp][:], AF.Silu),
                             inc="g_act")
                        P.wait("vector", "u_mm")
                        P.wait("vector", "gu_mul", P.C["gu_mul"])
                        P.op("vector", lambda E, pp=pp: E.tensor_copy(up_t[:], ps_u[pp][:]), inc="u_cp")
                        gb = (hc * nqt + qt) % 2
                        P.wait("vector", "g_act")
                        P.wait("vector", "u_cp")
                        P.wait("vector", f"gu_st{gb}")
                        P.op("vector", lambda E, gb=gb: E.tensor_tensor(
                            gu_t[gb][:], gate_t[:], up_t[:], op=ALU.mult), inc="gu_mul")
                        P.wait("gpsimd", "gu_mul")
                        P.dma("gpsimd", gu_s[qt, :, hc * 512:(hc + 1) * 512], gu_t[gb][:], inc=f"gu_st{gb}")
                P.wait("gpsimd", "gu_st0")
                P.wait("gpsimd", "gu_st1")
                P.wait("gpsimd", "wd_st0")
                P.wait("gpsimd", "wd_st1")

            return p4a

        def make_p4b(hp, c4):
            sb4 = lambda nm, sh, dt: c4.enter_context(nc.sbuf_tensor(nm + f"_h{hp}", sh, dt))
            wd_bf = [sb4(f"wd_bf{i}", [128, (nhc // 2) * 512], BF16) for i in range(3)]
            gu_sl = [sb4(f"gu_sl{i}", [128, (nhc // 4) * 512], BF16) for i in range(3)]
            r2st = [sb4(f"r2st{i}", [128, 512], F32) for i in range(4)]
            ps_d = [c4.enter_context(nc.psum_tensor(f"ps_d{i}_h{hp}", [128, 512], F32)) for i in range(8)]
            nsl = nhc // 4
            nhf = nhc // 2

            def p4b(P):
                for et in range(4):
                    for half in range(2):
                        h5 = et * 2 + half
                        P.wait("sync", "d_half", P.C["d_half"] - (16 if half == 0 else 2))
                        P.dma("sync", wd_bf[h5 % 3][:],
                              wd_s[et, :, half * nhf * 512:(half + 1) * nhf * 512],
                              inc=f"wdb_ld{h5 % 3}")
                    for qt in range(nqt):
                        it3 = et * nqt + qt
                        dset = (it3 % 2) * 4
                        P.wait("tensor", "r2_ev", P.C["r2_ev"] - 1)
                        for u in range(4):
                            for g in range(4):
                                cix = et * 4 + g
                                P.op("tensor", lambda E, u=u, g=g, cix=cix, qt=qt, dset=dset: E.matmul(
                                    ps_d[dset + u][:, g * 128:(g + 1) * 128],
                                    avt[:, cix * S_half + qt * 512 + u * 128: cix * S_half + qt * 512 + (u + 1) * 128],
                                    ident[:], start=(g == 0), stop=False),
                                    inc=None)
                        for sl in range(4):
                            g3 = it3 * 4 + sl
                            h5 = et * 2 + sl // 2
                            P.wait("sync", "d_half", P.C["d_half"] - 2)
                            P.dma("sync", gu_sl[g3 % 3][:],
                                  gu_s[qt, :, sl * nsl * 512:(sl + 1) * nsl * 512],
                                  inc=f"gu_ld{g3 % 3}")
                            P.wait("tensor", f"gu_ld{g3 % 3}")
                            P.wait("tensor", f"wdb_ld{h5 % 3}")
                            for ci in range(nsl):
                                hc = sl * nsl + ci
                                for u in range(4):
                                    P.op("tensor", lambda E, ci=ci, u=u, hc=hc, g3=g3, h5=h5, dset=dset: E.matmul(
                                        ps_d[dset + u][:],
                                        gu_sl[g3 % 3][:, ci * 512 + u * 128: ci * 512 + (u + 1) * 128],
                                        wd_bf[h5 % 3][:, (hc - (h5 % 2) * nhf) * 512:(hc - (h5 % 2) * nhf + 1) * 512],
                                        start=False, stop=(hc == nhc - 1)),
                                        inc=("d_half" if (ci == nsl - 1 and u == 3) else None))
                        P.wait("vector", "d_half")
                        P.wait("vector", "r2_st")
                        for u in range(4):
                            P.op("vector", lambda E, u=u, dset=dset: E.tensor_copy(r2st[u][:], ps_d[dset + u][:]),
                                 inc=("r2_ev" if u == 3 else None))
                        P.wait("gpsimd", "r2_ev")
                        for u in range(4):
                            P.dma("gpsimd", r2_s[qt * 512 + u * 128: qt * 512 + (u + 1) * 128, et * 512:(et + 1) * 512],
                                  r2st[u][:], inc="r2_st")
                P.wait("gpsimd", "r2_st")

            return p4b

        def make_p4c(hp, c4):
            sb4 = lambda nm, sh, dt: c4.enter_context(nc.sbuf_tensor(nm + f"_h{hp}", sh, dt))
            r2 = [sb4(f"r2_{i}", [128, D], F32) for i in range(2)]
            r2sq = sb4("r2sq", [128, D], F32)
            o_t = [sb4(f"o_t{i}", [128, D], F32) for i in range(2)]
            st_ln = sb4("st_ln2", [128, 8], F32)

            def p4c(P):
                for qc in range(nqc):
                    b = qc % 2
                    P.wait("sync", "o_done", P.C["o_done"] - 1)
                    P.dma("sync", r2[b][:], r2_s[qc * 128:(qc + 1) * 128, :], inc=f"r2_ld{b}")
                    P.wait("scalar", f"r2_ld{b}")
                    P.wait("scalar", "ln2_mid")
                    P.op("scalar", lambda E, b=b: E.activation(r2sq[:], r2[b][:], AF.Copy, accum_out=st_ln[:, 0:1]),
                         inc="r2sq_done")
                    P.wait("scalar", "r2sq_done")
                    P.op("scalar", lambda E, b=b: E.activation(r2sq[:], r2[b][:], AF.Square, accum_out=st_ln[:, 1:2]),
                         inc="ln2_sum")
                    P.wait("vector", "ln2_sum")
                    P.op("vector", lambda E: E.tensor_scalar(st_ln[:, 2:3], st_ln[:, 0:1], 1.0 / D, None, op0=ALU.mult), inc="ln2_mid")
                    P.wait("vector", "ln2_mid")
                    P.op("vector", lambda E: E.tensor_tensor(st_ln[:, 3:4], st_ln[:, 2:3], st_ln[:, 2:3], op=ALU.mult), inc="ln2_mid")
                    P.wait("vector", "ln2_mid")
                    P.wait("vector", "ln2_sqrt")
                    P.op("vector", lambda E: E.scalar_tensor_tensor(
                        st_ln[:, 4:5], st_ln[:, 1:2], 1.0 / D, st_ln[:, 3:4], op0=ALU.mult, op1=ALU.subtract),
                        inc="ln2_var")
                    P.wait("scalar", "ln2_var")
                    P.op("scalar", lambda E: E.activation(st_ln[:, 5:6], st_ln[:, 4:5], AF.Sqrt, bias=eps_col[:]),
                         inc="ln2_sqrt")
                    P.wait("vector", "ln2_sqrt")
                    P.op("vector", lambda E: E.reciprocal(st_ln[:, 6:7], st_ln[:, 5:6]), inc="ln2_rstd")
                    P.wait("vector", "ln2_rstd")
                    P.op("vector", lambda E: E.scalar_tensor_tensor(
                        st_ln[:, 7:8], st_ln[:, 2:3], -1.0, st_ln[:, 6:7], op0=ALU.mult, op1=ALU.mult),
                        inc="ln2_nmr")
                    P.wait("vector", "ln2_nmr")
                    P.wait("vector", f"o_st{b}")
                    P.op("vector", lambda E, b=b: E.tensor_scalar(
                        o_t[b][:], r2[b][:], st_ln[:, 6:7], st_ln[:, 7:8], op0=ALU.mult, op1=ALU.add),
                        inc="o_done")
                    P.wait("gpsimd", "o_done")
                    P.dma("gpsimd", out_d[hp * S_half + qc * 128: hp * S_half + (qc + 1) * 128, :],
                          o_t[b][:], inc=f"o_st{b}")
                P.wait("gpsimd", "o_st0")
                P.wait("gpsimd", "o_st1")

            return p4c

        for hp in range(2):
            with ExitStack() as c2:
                p2 = make_p2(hp, c2)
                P.phase(f"p2_h{hp}", p2)
            if debug_stop == "p2":
                with ExitStack() as cd:
                    st = cd.enter_context(nc.sbuf_tensor("avdbg_st", [128, S_half], BF16))

                    def dump(P):
                        for cix in range(NCH):
                            P.wait("vector", "d_st")
                            P.op("vector", lambda E, cix=cix: E.tensor_copy(
                                st[:], avt[:, cix * S_half:(cix + 1) * S_half]), inc="d_cp")
                            P.wait("gpsimd", "d_cp")
                            P.dma("gpsimd", avt_dbg[cix * 128:(cix + 1) * 128, :], st[:], inc="d_st")
                        P.wait("gpsimd", "d_st")
                    P.phase("dump", dump)
                return nc
            with ExitStack() as c3:
                p3 = make_p3(hp, c3)
                P.phase(f"p3_h{hp}", p3)
            if debug_stop == "p3":
                with ExitStack() as cd:
                    st = cd.enter_context(nc.sbuf_tensor("z1dbg_st", [128, S_half], BF16))

                    def dump(P):
                        for cix in range(NCH):
                            P.wait("vector", "d_st")
                            P.op("vector", lambda E, cix=cix: E.tensor_copy(
                                st[:], avt[:, cix * S_half:(cix + 1) * S_half]), inc="d_cp")
                            P.wait("gpsimd", "d_cp")
                            P.dma("gpsimd", z1t_dbg[cix * 128:(cix + 1) * 128, :], st[:], inc="d_st")
                        P.wait("gpsimd", "d_st")
                    P.phase("dump", dump)
                return nc
            with ExitStack() as c4:
                p4a = make_p4a(hp, c4)
                P.phase(f"p4a_h{hp}", p4a)
            with ExitStack() as c4:
                p4b = make_p4b(hp, c4)
                P.phase(f"p4b_h{hp}", p4b)
            if debug_stop == "p4b" and hp == 0:
                return nc
            with ExitStack() as c4:
                p4c = make_p4c(hp, c4)
                P.phase(f"p4c_h{hp}", p4c)

        return nc
    return nc


_T = lambda a: np.ascontiguousarray(np.asarray(a).T)


def host_layout(inputs, hid=HID):
    L = {}
    L["wqt"] = _T(inputs["L0_attn_Wq"])
    L["wkt"] = _T(inputs["L0_attn_Wk"])
    L["wvt"] = _T(inputs["L0_attn_Wv"])
    L["wot"] = _T(inputs["L0_attn_Wo"])
    L["bq"] = _T(np.asarray(inputs["L0_attn_bq"], dtype=np.float32).reshape(NCH, 128))
    L["bk"] = _T(np.asarray(inputs["L0_attn_bk"], dtype=np.float32).reshape(NCH, 128))
    L["bv"] = _T(np.asarray(inputs["L0_attn_bv"], dtype=np.float32).reshape(NCH, 128))
    L["bo_row"] = np.ascontiguousarray(np.asarray(inputs["L0_attn_bo"], dtype=np.float32).reshape(1, D))
    wg = np.asarray(inputs["L0_ffn_wq_gate"])
    wu = np.asarray(inputs["L0_ffn_wq_up"])
    L["wgt_r"] = np.ascontiguousarray(
        wg.reshape(hid // 128, 128, NCH, 128).transpose(0, 3, 2, 1).reshape(hid // 128, 128, NCH * 128))
    L["wut_r"] = np.ascontiguousarray(
        wu.reshape(hid // 128, 128, NCH, 128).transpose(0, 3, 2, 1).reshape(hid // 128, 128, NCH * 128))
    L["wdt"] = _T(inputs["L0_ffn_wq_down"])
    L["sg_row"] = np.ascontiguousarray(np.asarray(inputs["L0_ffn_s_gate"], dtype=np.float32).reshape(1, hid))
    L["su_row"] = np.ascontiguousarray(np.asarray(inputs["L0_ffn_s_up"], dtype=np.float32).reshape(1, hid))
    L["sd_row"] = np.ascontiguousarray(np.asarray(inputs["L0_ffn_s_down"], dtype=np.float32).reshape(1, D))
    L["ident"] = np.eye(128, dtype=np.float32)
    L["ones_in"] = np.ones((128, 128), dtype=np.float32)
    mask = np.zeros((128, 512), dtype=np.float32)
    qq = np.arange(256)
    for j in range(2):
        for kk in range(128):
            mask[kk, j * 256:(j + 1) * 256] = (qq >= j * 128 + kk).astype(np.float32)
    L["mask"] = mask
    return L


def kernel(**inputs):
    X = np.asarray(inputs["L0_attn_X"], dtype=np.float32)
    B, S, d = X.shape
    assert d == D
    L = host_layout(inputs)
    nc = build_program(S=S)
    in_maps = []
    for c in range(B):
        m = dict(L)
        m["xtok"] = np.ascontiguousarray(X[c])
        m["xt"] = _T(X[c])
        in_maps.append(m)
    res = run_bass_kernel_spmd(nc, in_maps, core_ids=list(range(B)))
    return np.stack([res.results[c]["out"] for c in range(B)], axis=0)


# version 15
# speedup vs baseline: 1.0555x; 1.0039x over previous
import numpy as np
from collections import defaultdict
import concourse.bass as bass
import concourse.mybir as mybir
from concourse.bass_utils import run_bass_kernel_spmd
from contextlib import ExitStack

F32 = mybir.dt.float32
BF16 = mybir.dt.bfloat16
I32 = mybir.dt.int32
AF = mybir.ActivationFunctionType
ALU = mybir.AluOpType
AX = mybir.AxisListType

ENGINES = ("sync", "tensor", "scalar", "vector", "gpsimd")

D = 2048
NCH = D // 128
H = 16
DH = 128
HID = 5632
QK_SCALE = 1.0 / np.sqrt(np.sqrt(DH))
EPS = 1e-5


class Prog:

    def __init__(self, nc, ctx):
        self.nc = nc
        self.ctx = ctx
        self.C = defaultdict(int)
        self.sems = {}
        self.cur = None
        self.E = None

    def sem(self, name):
        if name not in self.sems:
            self.sems[name] = self.ctx.enter_context(self.nc.semaphore(name))
        self.C.setdefault(name, 0)
        return self.sems[name]

    def phase(self, name, fn):
        snap = dict(self.C)
        final = None
        with self.nc.Block(name) as blk:
            for eng in ENGINES:
                self.C = defaultdict(int, snap)
                self.cur = eng

                def run(E, fn=fn):
                    self.E = E
                    fn(self)

                getattr(blk, eng)(run)
                if final is None:
                    final = dict(self.C)
                else:
                    assert final == dict(self.C), f"phase {name}: nondeterministic emission ({eng})"
        self.C = defaultdict(int, final)

    def wait(self, eng, sem, val=None):
        self.sem(sem)
        if val is None:
            val = self.C[sem]
        if val <= 0:
            return
        if self.cur == eng:
            self.E.wait_ge(self.sems[sem], val)

    def op(self, eng, emit, inc=None, inc_n=1):
        r = None
        if self.cur == eng:
            r = emit(self.E)
        if inc is not None:
            self.sem(inc)
            self.C[inc] += inc_n
            if r is not None:
                r.then_inc(self.sems[inc], inc_n)
        return r

    def dma(self, eng, dst, src, inc):
        self.op(eng, lambda E: E.dma_start(dst, src), inc=inc, inc_n=16)


def build_program(S=4096, hid=HID, debug_stop=None):
    nhc = hid // 128
    S_half = S // 2
    nqt = S_half // 512
    nqc = S_half // 128
    assert S % 1024 == 0

    nc = bass.Bass("TRN2", target_bir_lowering=False)

    xt_d = nc.dram_tensor("xt", [D, S], F32, kind="ExternalInput")
    xtok_d = nc.dram_tensor("xtok", [S, D], F32, kind="ExternalInput")
    wqt_d = nc.dram_tensor("wqt", [D, D], F32, kind="ExternalInput")
    wkt_d = nc.dram_tensor("wkt", [D, D], F32, kind="ExternalInput")
    wvt_d = nc.dram_tensor("wvt", [D, D], F32, kind="ExternalInput")
    wot_d = nc.dram_tensor("wot", [D, D], F32, kind="ExternalInput")
    bq_d = nc.dram_tensor("bq", [128, NCH], F32, kind="ExternalInput")
    bk_d = nc.dram_tensor("bk", [128, NCH], F32, kind="ExternalInput")
    bv_d = nc.dram_tensor("bv", [128, NCH], F32, kind="ExternalInput")
    bo_d = nc.dram_tensor("bo_row", [1, D], F32, kind="ExternalInput")
    wgt_d = nc.dram_tensor("wgt_r", [hid // 128, 128, D], I32, kind="ExternalInput")
    wut_d = nc.dram_tensor("wut_r", [hid // 128, 128, D], I32, kind="ExternalInput")
    wdt_d = nc.dram_tensor("wdt", [hid, D], I32, kind="ExternalInput")
    sg_d = nc.dram_tensor("sg_row", [1, hid], F32, kind="ExternalInput")
    su_d = nc.dram_tensor("su_row", [1, hid], F32, kind="ExternalInput")
    sd_d = nc.dram_tensor("sd_row", [1, D], F32, kind="ExternalInput")
    ident_d = nc.dram_tensor("ident", [128, 128], F32, kind="ExternalInput")
    ones_d = nc.dram_tensor("ones_in", [128, 128], F32, kind="ExternalInput")
    mask_d = nc.dram_tensor("mask", [128, 512], F32, kind="ExternalInput")

    out_d = nc.dram_tensor("out", [S, D], F32, kind="ExternalOutput")

    dbg = lambda want: "ExternalOutput" if (debug_stop == want) else "Internal"
    qt_s = nc.dram_tensor("qt_s", [D, S], BF16, kind=dbg("p1"))
    kt_s = nc.dram_tensor("kt_s", [D, S], BF16, kind=dbg("p1"))
    vt_s = nc.dram_tensor("vt_s", [D, S], BF16, kind=dbg("p1"))
    avt_dbg = nc.dram_tensor("avt_dbg", [D, S_half], BF16, kind="ExternalOutput") if debug_stop == "p2" else None
    z1t_dbg = nc.dram_tensor("z1t_dbg", [D, S_half], BF16, kind="ExternalOutput") if debug_stop == "p3" else None
    gu_s = nc.dram_tensor("gu_s", [S_half // 512, 128, nhc * 512], BF16, kind="Internal")
    wd_s = nc.dram_tensor("wd_s", [4, 128, nhc * 512], BF16, kind="Internal")
    r2_s = nc.dram_tensor("r2_s", [S_half, D], F32, kind=dbg("p4b"))

    ctx = ExitStack()
    with ctx:
        P = Prog(nc, ctx)
        sb = lambda nm, sh, dt: ctx.enter_context(nc.sbuf_tensor(nm, sh, dt))

        ident = sb("ident_sb", [128, 128], BF16)
        ones_col = sb("ones_col", [128, 1], BF16)
        ones_row = sb("ones_row", [1, 128], BF16)
        bo_bf = sb("bo_bf", [1, D], BF16)
        eps_col = sb("eps_col", [128, 1], F32)
        sbq = sb("sbq", [128, NCH], F32)
        sbk = sb("sbk", [128, NCH], F32)
        sbv = sb("sbv", [128, NCH], F32)
        mask_sb = sb("mask_sb", [128, 512], BF16)

        c0 = ExitStack()
        cst = c0.enter_context(nc.sbuf_tensor("cst", [128, 2048], F32))
        bo_row = c0.enter_context(nc.sbuf_tensor("bo_row_sb", [1, D], F32))

        def p0(P):
            P.dma("sync", cst[:, 0:128], ident_d[:], inc="c_ld")
            P.dma("sync", cst[:, 128:256], ones_d[:], inc="c_ld")
            P.dma("sync", bo_row[:], bo_d[:], inc="c_ld")
            P.dma("sync", sbq[:], bq_d[:], inc="c_ld")
            P.dma("sync", sbk[:], bk_d[:], inc="c_ld")
            P.dma("sync", sbv[:], bv_d[:], inc="c_ld")
            P.wait("vector", "c_ld")
            P.op("vector", lambda E: E.tensor_copy(ident[:], cst[:, 0:128]), inc="c_rdy")
            P.op("vector", lambda E: E.tensor_copy(ones_col[:], cst[:, 128:129]), inc="c_rdy")
            P.op("vector", lambda E: E.tensor_copy(ones_row[:], cst[0:1, 128:256]), inc="c_rdy")
            P.op("vector", lambda E: E.tensor_scalar(sbq[:], sbq[:], float(QK_SCALE), None, op0=ALU.mult), inc="c_rdy")
            P.op("vector", lambda E: E.tensor_scalar(sbk[:], sbk[:], float(QK_SCALE), None, op0=ALU.mult), inc="c_rdy")
            P.op("vector", lambda E: E.tensor_copy(bo_bf[:], bo_row[:]), inc="c_rdy")
            P.op("vector", lambda E: E.memset(eps_col[:], float(EPS)), inc="c_rdy")
            P.wait("sync", "c_rdy")
            P.dma("sync", cst[:, 0:512], mask_d[:], inc="c_ld")
            P.wait("vector", "c_ld")
            P.op("vector", lambda E: E.tensor_copy(mask_sb[:], cst[:, 0:512]), inc="c_rdy")

        P.phase("p0", p0)
        c0.close()
        NC_RDY = P.C["c_rdy"]

        with ExitStack() as c1:
            sb1 = lambda nm, sh, dt: c1.enter_context(nc.sbuf_tensor(nm, sh, dt))
            xth = sb1("xth", [128, NCH * S_half], BF16)
            w_half = [sb1(f"w_half{i}", [128, (NCH // 2) * D], BF16) for i in range(3)]
            o_stage = [sb1(f"o_stage{i}", [128, S_half], BF16) for i in range(2)]
            ps1 = [c1.enter_context(nc.psum_tensor(f"ps1_{i}", [128, 512], F32)) for i in range(2)]

            projs = [(kt_s, wkt_d, sbk, True), (vt_s, wvt_d, sbv, False), (qt_s, wqt_d, sbq, True)]

            def p1(P, hp):
                for cix in range(NCH):
                    P.dma("gpsimd", xth[:, cix * S_half:(cix + 1) * S_half],
                          xt_d[cix * 128:(cix + 1) * 128, hp * S_half:(hp + 1) * S_half],
                          inc="x_ld")
                xc = P.C["x_ld"]
                for pj, (scr, w_d, bias_t, scaled) in enumerate(projs):
                    gpj = P.C["p1_w5"]
                    for lo in range(2):
                        h5 = gpj + lo
                        need = ((h5 - 3) // 2 + 1) * NCH * nqt
                        P.wait("gpsimd", "p1_mm", min(need, P.C["p1_mm"]) if h5 >= 3 else 0)
                        for cq in range(NCH // 2):
                            cix = lo * (NCH // 2) + cq
                            P.dma("gpsimd", w_half[h5 % 3][:, cq * D:(cq + 1) * D],
                                  w_d[cix * 128:(cix + 1) * 128, :], inc=f"w_ld{h5 % 3}")
                        P.op("gpsimd", lambda E: None, inc="p1_w5")
                    W_RDY0 = P.C[f"w_ld{gpj % 3}"]
                    W_RDY1 = P.C[f"w_ld{(gpj + 1) % 3}"]
                    for ec in range(NCH):
                        for qt in range(nqt):
                            pp = (pj * NCH * nqt + ec * nqt + qt) % 2
                            if ec == 0 and qt == 0:
                                P.wait("tensor", f"w_ld{gpj % 3}", W_RDY0)
                                P.wait("tensor", f"w_ld{(gpj + 1) % 3}", W_RDY1)
                                if pj == 0:
                                    P.wait("tensor", "x_ld", xc)
                                    P.wait("tensor", "c_rdy", NC_RDY)
                                    P.wait("scalar", "c_rdy", NC_RDY)
                            P.wait("tensor", "p1_ev", P.C["p1_ev"] - 1)
                            for cix in range(NCH):
                                wb = w_half[(gpj + cix // (NCH // 2)) % 3]
                                cq = cix % (NCH // 2)
                                P.op("tensor", lambda E, cix=cix, cq=cq, wb=wb, ec=ec, pp=pp, qt=qt: E.matmul(
                                    ps1[pp][:],
                                    wb[:, cq * D + ec * 128: cq * D + ec * 128 + 128],
                                    xth[:, cix * S_half + qt * 512: cix * S_half + qt * 512 + 512],
                                    start=(cix == 0), stop=(cix == NCH - 1)),
                                    inc=("p1_mm" if cix == NCH - 1 else None))
                            b = (pj * NCH + ec) % 2
                            P.wait("scalar", "p1_mm")
                            if qt == 0:
                                P.wait("scalar", f"p1_st{b}")
                            P.op("scalar", lambda E, b=b, pp=pp, qt=qt, ec=ec, bias_t=bias_t, scaled=scaled:
                                 E.activation(o_stage[b][:, qt * 512:qt * 512 + 512], ps1[pp][:],
                                              AF.Identity,
                                              bias=bias_t[:, ec:ec + 1],
                                              scale=float(QK_SCALE) if scaled else 1.0),
                                 inc="p1_ev")
                        P.wait("sync", "p1_ev")
                        P.dma("sync", scr[ec * 128:(ec + 1) * 128, hp * S_half:(hp + 1) * S_half],
                              o_stage[b][:], inc=f"p1_st{b}")
                P.wait("sync", "p1_st0")
                P.wait("sync", "p1_st1")

            P.phase("p1_h0", lambda P: p1(P, 0))
            P.phase("p1_h1", lambda P: p1(P, 1))

        if debug_stop == "p1":
            return nc

        avt = sb("avt", [128, NCH * S_half], BF16)

        def make_p2(hp, c2, tail=None):
            sb2 = lambda nm, sh, dt: c2.enter_context(nc.sbuf_tensor(nm + f"_h{hp}", sh, dt))
            SK = (hp + 1) * S_half
            nks = SK // 128
            QT2 = 256
            nqt2 = S_half // QT2
            nsub = S_half // 128
            k_sb = [sb2(f"k_sb{i}", [128, SK], BF16) for i in range(3)]
            vt_sb = [sb2(f"vt_sb{i}", [128, SK], BF16) for i in range(3)]
            q_sb = [sb2(f"q_sb{i}", [128, S_half], BF16) for i in range(3)]
            vaug = sb2("vaug", [128, nks * 130], BF16)
            expS = [sb2(f"expS{i}", [128, 1024], BF16) for i in range(3)]
            sqtmp = sb2("sqtmp", [128, S_half + SK], BF16)
            avtok_all = sb2("avtok_all", [128, nsub * 128], BF16)
            sums_all = sb2("sums_all", [128, nsub], F32)
            rec_all = sb2("rec_all", [128, nsub], F32)
            avtok_n = [sb2(f"avtok_n{i}", [128, 128], BF16) for i in range(2)]
            c_all = sb2("c_all", [128, H], F32)
            nrm = sb2("nrm", [1, 8], F32)
            nrm_bf = sb2("nrm_bf", [1, 1], BF16)
            rec = sb2("rec", [128, 2], F32)
            ps_s = [c2.enter_context(nc.psum_tensor(f"ps_s{i}_h{hp}", [128, 1024], F32)) for i in range(3)]
            ps_av = [c2.enter_context(nc.psum_tensor(f"ps_av{i}_h{hp}", [128, 512], F32)) for i in range(2)]

            def p2(P):
                P.op("vector", lambda E: E.memset(vaug[:], 1.0), inc="vaug_init")
                for h in range(H):
                    hb = h % 3
                    P.wait("sync", "head_done", P.C["head_done"] - 2)
                    P.wait("sync", "sq_k", P.C["sq_k"] - 2)
                    P.dma("sync", k_sb[hb][:], kt_s[h * 128:(h + 1) * 128, 0:SK], inc=f"kv_ld{hb}")
                    P.dma("sync", vt_sb[hb][:], vt_s[h * 128:(h + 1) * 128, 0:SK], inc=f"kv_ld{hb}")
                    P.dma("sync", q_sb[hb][:], qt_s[h * 128:(h + 1) * 128, hp * S_half:(hp + 1) * S_half], inc=f"kv_ld{hb}")
                    P.wait("tensor", f"kv_ld{hb}")

                    for g in range(nks // 4):
                        gb = g % 2
                        P.wait("tensor", "vtr_ev", P.C["vtr_ev"] - 1)
                        if g < 2:
                            P.wait("tensor", "avt_ev")
                        for j in range(4):
                            P.op("tensor", lambda E, g=g, j=j, gb=gb, hb=hb: E.matmul(
                                ps_av[gb][:, j * 128:(j + 1) * 128],
                                vt_sb[hb][:, (g * 4 + j) * 128:(g * 4 + j + 1) * 128],
                                ident[:], start=True, stop=True),
                                inc=("vtr_mm" if j == 3 else None))
                        P.wait("vector", "vtr_mm")
                        P.wait("vector", "vaug_init")
                        P.wait("vector", "avt_ev")
                        P.op("vector", lambda E, g=g, gb=gb: E.tensor_copy(
                            vaug[:, :].rearrange("p (k c) -> p k c", c=130)[:, g * 4:(g + 1) * 4, 0:128],
                            ps_av[gb][:, :].rearrange("p (k c) -> p k c", c=128)),
                            inc="vtr_ev")

                    P.wait("scalar", "nrm_mm")
                    P.wait("scalar", f"kv_ld{hb}")
                    P.op("scalar", lambda E, hb=hb: E.activation(sqtmp[:, 0:S_half], q_sb[hb][:], AF.Square),
                         inc="sq_q")
                    P.op("scalar", lambda E, hb=hb: E.activation(sqtmp[:, S_half:S_half + SK], k_sb[hb][:], AF.Square),
                         inc="sq_k")
                    P.wait("tensor", "vtr_ev")
                    P.wait("tensor", "vaug_init")
                    P.wait("tensor", "exp_done")
                    P.wait("tensor", "sq_q")
                    P.wait("tensor", "sq_k")
                    nt = (S_half + SK) // 512
                    for t in range(nt):
                        s6 = t % 6
                        P.wait("tensor", "nrm_red", P.C["nrm_red"] - 5)
                        P.op("tensor", lambda E, t=t, s6=s6: E.matmul(
                            ps_s[s6 // 2][0:1, (s6 % 2) * 512:(s6 % 2) * 512 + 512],
                            ones_col[:], sqtmp[:, t * 512:(t + 1) * 512],
                            start=True, stop=True), inc="nrm_mm")
                        P.wait("vector", "nrm_mm")
                        P.op("vector", lambda E, t=t, s6=s6: E.tensor_reduce(
                            nrm[0:1, 2 + (t % 2):3 + (t % 2)],
                            ps_s[s6 // 2][0:1, (s6 % 2) * 512:(s6 % 2) * 512 + 512], AX.X, ALU.max),
                            inc="nrm_red")
                        dst = 0 if t < S_half // 512 else 1
                        first = (t == 0) or (t == S_half // 512)
                        P.wait("vector", "nrm_red")
                        if first:
                            P.op("vector", lambda E, dst=dst, t=t: E.tensor_copy(
                                nrm[0:1, dst:dst + 1], nrm[0:1, 2 + (t % 2):3 + (t % 2)]), inc="nrm_acc")
                        else:
                            P.op("vector", lambda E, dst=dst, t=t: E.tensor_tensor(
                                nrm[0:1, dst:dst + 1], nrm[0:1, dst:dst + 1],
                                nrm[0:1, 2 + (t % 2):3 + (t % 2)], op=ALU.max), inc="nrm_acc")
                    P.wait("vector", "nrm_acc")
                    P.op("vector", lambda E: E.tensor_tensor(nrm[0:1, 4:5], nrm[0:1, 0:1], nrm[0:1, 1:2], op=ALU.mult),
                         inc="nrm_c")
                    P.wait("scalar", "nrm_c")
                    P.op("scalar", lambda E: E.activation(nrm[0:1, 5:6], nrm[0:1, 4:5], AF.Sqrt), inc="nrm_s")
                    P.wait("vector", "nrm_s")
                    P.op("vector", lambda E: E.tensor_scalar(nrm_bf[:], nrm[0:1, 5:6], -1.0, None, op0=ALU.mult),
                         inc="nrm_n")
                    P.wait("tensor", "nrm_n")
                    P.wait("tensor", "avt_ev")
                    P.wait("tensor", "vtr_ev")
                    P.op("tensor", lambda E: E.matmul(ps_av[0][:, 0:1], ones_row[:], nrm_bf[:],
                                                      start=True, stop=True), inc="nrm_b")
                    P.wait("vector", "exp_done")
                    P.wait("vector", "nrm_b")
                    P.op("vector", lambda E, h=h: E.tensor_copy(c_all[:, h:h + 1], ps_av[0][:, 0:1]), inc="c_rdy2")

                    glist = []
                    for qt in range(nqt2):
                        gi2 = hp * nqt2 + qt
                        nk = 2 * (gi2 + 1)
                        for s in range(0, nk, 4):
                            glist.append((qt, gi2, nk, list(range(s, min(s + 4, nk)))))

                    def emit_scores(i):
                        qt, gi2, nk, grp = glist[i]
                        par = P.C["sc_grp"] % 3
                        P.wait("tensor", "exp_done", P.C["sc_grp"] - 2)
                        if i == 0:
                            P.wait("tensor", "c_rdy2")
                            P.wait("scalar", "c_rdy2")
                        for jj, ks in enumerate(grp):
                            P.op("tensor", lambda E, jj=jj, ks=ks, par=par, hb=hb, qt=qt: E.matmul(
                                ps_s[par][:, jj * 256:(jj + 1) * 256],
                                k_sb[hb][:, ks * 128:(ks + 1) * 128],
                                q_sb[hb][:, qt * 256:(qt + 1) * 256],
                                start=True, stop=True),
                                inc=("sc_grp" if jj == len(grp) - 1 else None))

                    def emit_av(i):
                        qt, gi2, nk, grp = glist[i]
                        par = (P.C["av_grp"]) % 3
                        P.wait("scalar", "sc_grp", P.C["av_grp"] + 1)
                        P.wait("scalar", "av_grp", P.C["av_grp"] - 2)
                        P.op("scalar", lambda E, grp=grp, par=par, h=h: E.activation(
                            expS[par][:, 0:256 * len(grp)], ps_s[par][:, 0:256 * len(grp)],
                            AF.Exp, bias=c_all[:, h:h + 1], scale=1.0), inc="exp_done")
                        dmask = [(jj, ks) for jj, ks in enumerate(grp) if ks >= 2 * gi2]
                        if dmask:
                            P.wait("vector", "exp_done")
                            for jj, ks in dmask:
                                P.op("vector", lambda E, jj=jj, j=ks - 2 * gi2, par=par: E.tensor_tensor(
                                    expS[par][:, jj * 256:(jj + 1) * 256],
                                    expS[par][:, jj * 256:(jj + 1) * 256],
                                    mask_sb[:, j * 256:(j + 1) * 256], op=ALU.mult),
                                    inc=("mask_done" if (jj, ks) == dmask[-1] else None))
                        P.wait("tensor", "exp_done")
                        if dmask:
                            P.wait("tensor", "mask_done")
                        if grp[0] == 0:
                            P.wait("tensor", "av_ev")
                        for jj, ks in enumerate(grp):
                            for u in range(2):
                                P.op("tensor", lambda E, jj=jj, ks=ks, u=u, par=par, nk=nk: E.matmul(
                                    ps_av[u][:, 0:129],
                                    expS[par][:, jj * 256 + u * 128: jj * 256 + (u + 1) * 128],
                                    vaug[:, ks * 130: ks * 130 + 129],
                                    start=(ks == 0), stop=(ks == nk - 1)),
                                    inc=("av_grp" if (u == 1 and jj == len(grp) - 1) else None))
                        if ks == nk - 1:
                            P.wait("vector", "av_grp")
                            if qt == 0:
                                P.wait("vector", "avt_mm")
                            for u in range(2):
                                P.op("vector", lambda E, u=u: E.reciprocal(
                                    rec[:, u:u + 1], ps_av[u][:, 128:129]),
                                    inc=("rec_done" if u == 1 else None))
                            P.wait("vector", "rec_done")
                            for u in range(2):
                                P.op("vector", lambda E, u=u, qt=qt: E.tensor_scalar(
                                    avtok_all[:, (qt * 2 + u) * 128:(qt * 2 + u + 1) * 128],
                                    ps_av[u][:, 0:128],
                                    rec[:, u:u + 1], None, op0=ALU.mult),
                                    inc=("av_ev" if u == 1 else None))

                    emit_scores(0)
                    if len(glist) > 1:
                        emit_scores(1)
                    for i in range(len(glist)):
                        if i + 2 < len(glist):
                            emit_scores(i + 2)
                        emit_av(i)

                    P.wait("tensor", "av_ev")
                    for u in range(nsub):
                        P.wait("tensor", "avt_ev", P.C["avt_ev"] - 1)
                        P.op("tensor", lambda E, u=u: E.matmul(
                            ps_av[u % 2][:, 0:128],
                            avtok_all[:, u * 128:(u + 1) * 128], ident[:],
                            start=True, stop=True), inc="avt_mm")
                        P.wait("vector", "avt_mm")
                        P.op("vector", lambda E, u=u, h=h: E.tensor_copy(
                            avt[:, h * S_half + u * 128: h * S_half + (u + 1) * 128],
                            ps_av[u % 2][:, 0:128]),
                            inc=("avt_ev"))
                    P.op("vector", lambda E: E.memset(rec[:, 0:1], 0.0), inc="head_done")
                    if tail is not None and h < nqc:
                        tail(P, h)
                if tail is not None:
                    for qc in range(H, nqc):
                        tail(P, qc)

            return p2

        def make_p3(hp, c3):
            sb3 = lambda nm, sh, dt: c3.enter_context(nc.sbuf_tensor(nm + f"_h{hp}", sh, dt))
            wo_bf = sb3("wo_bf", [128, NCH * D], BF16)
            x_tok = [sb3(f"x_tok{i}", [128, D], F32) for i in range(2)]
            r1 = sb3("r1", [128, D], F32)
            r1sq = sb3("r1sq", [128, D], F32)
            z1tok = [sb3(f"z1tok{i}", [128, D], BF16) for i in range(2)]
            st_ln = sb3("st_ln", [128, 8], F32)
            ps_o = c3.enter_context(nc.psum_tensor(f"ps_o_h{hp}", [128, 2048], F32))
            ps_z = c3.enter_context(nc.psum_tensor(f"ps_z_h{hp}", [128, 2048], F32))

            def p3(P):
                WO_RDY = {}
                for et in range(4):
                    for cix in range(NCH):
                        P.dma("gpsimd", wo_bf[:, cix * D + et * 512: cix * D + (et + 1) * 512],
                              wot_d[cix * 128:(cix + 1) * 128, et * 512:(et + 1) * 512], inc=f"wo_ld{et}")
                    WO_RDY[et] = P.C[f"wo_ld{et}"]

                def front(qc):
                    qg = hp * S_half + qc * 128
                    b = qc % 2
                    P.wait("sync", "r1_done", P.C["r1_done"] - 4 * 1)
                    P.dma("sync", x_tok[b][:], xtok_d[qg:qg + 128, :], inc=f"xr_ld{b}")
                    for et in range(4):
                        if qc == 0:
                            P.wait("tensor", f"wo_ld{et}", WO_RDY[et])
                        P.wait("tensor", "r1_done", P.C["r1_done"] - 2)
                        for cix in range(NCH):
                            P.op("tensor", lambda E, cix=cix, et=et, qc=qc: E.matmul(
                                ps_o[:, et * 512:(et + 1) * 512],
                                avt[:, cix * S_half + qc * 128: cix * S_half + (qc + 1) * 128],
                                wo_bf[:, cix * D + et * 512: cix * D + et * 512 + 512],
                                start=(cix == 0), stop=False),
                                inc=None)
                        P.op("tensor", lambda E, et=et: E.matmul(
                            ps_o[:, et * 512:(et + 1) * 512], ones_row[:], bo_bf[0:1, et * 512:(et + 1) * 512],
                            start=False, stop=True), inc="o_mm")
                        P.wait("vector", "o_mm")
                        P.wait("vector", f"xr_ld{b}")
                        if et == 0:
                            P.wait("vector", "r1sq_done")
                            P.wait("vector", "z1_done")
                        P.op("vector", lambda E, et=et, b=b: E.tensor_tensor(
                            r1[:, et * 512:(et + 1) * 512], ps_o[:, et * 512:(et + 1) * 512],
                            x_tok[b][:, et * 512:(et + 1) * 512], op=ALU.add), inc="r1_done")
                    P.wait("scalar", "r1_done")
                    P.wait("scalar", "ln_mid")
                    P.op("scalar", lambda E: E.activation(r1sq[:], r1[:], AF.Copy, accum_out=st_ln[:, 0:1]),
                         inc="r1sq_done")
                    P.wait("scalar", "r1sq_done")
                    P.op("scalar", lambda E: E.activation(r1sq[:], r1[:], AF.Square, accum_out=st_ln[:, 1:2]),
                         inc="ln_sum")
                    P.wait("vector", "ln_sum")
                    P.op("vector", lambda E: E.tensor_scalar(st_ln[:, 2:3], st_ln[:, 0:1], 1.0 / D, None, op0=ALU.mult), inc="ln_mid")
                    P.wait("vector", "ln_mid")
                    P.op("vector", lambda E: E.tensor_tensor(st_ln[:, 3:4], st_ln[:, 2:3], st_ln[:, 2:3], op=ALU.mult), inc="ln_mid")
                    P.wait("vector", "ln_mid")
                    P.wait("vector", "ln_sqrt")
                    P.op("vector", lambda E: E.scalar_tensor_tensor(
                        st_ln[:, 4:5], st_ln[:, 1:2], 1.0 / D, st_ln[:, 3:4], op0=ALU.mult, op1=ALU.subtract),
                        inc="ln_var")
                    P.wait("scalar", "ln_var")
                    P.op("scalar", lambda E: E.activation(st_ln[:, 5:6], st_ln[:, 4:5], AF.Sqrt, bias=eps_col[:]),
                         inc="ln_sqrt")
                    P.wait("vector", "ln_sqrt")
                    P.op("vector", lambda E: E.reciprocal(st_ln[:, 6:7], st_ln[:, 5:6]), inc="ln_rstd")
                    P.wait("vector", "ln_rstd")
                    P.op("vector", lambda E: E.scalar_tensor_tensor(
                        st_ln[:, 7:8], st_ln[:, 2:3], -1.0, st_ln[:, 6:7], op0=ALU.mult, op1=ALU.mult),
                        inc="ln_nmr")
                    P.wait("vector", "ln_nmr")
                    P.wait("vector", "z1_mm")
                    P.op("vector", lambda E, b=b: E.tensor_scalar(
                        z1tok[b][:], r1[:], st_ln[:, 6:7], st_ln[:, 7:8], op0=ALU.mult, op1=ALU.add),
                        inc="z1_done")

                def back(qc):
                    b = qc % 2
                    P.wait("tensor", "z1_done", P.C["z1_done"] - (1 if qc == nqc - 1 else 2) + 1)
                    P.wait("tensor", "z1_ev")
                    for cix in range(NCH):
                        P.op("tensor", lambda E, cix=cix, b=b: E.matmul(
                            ps_z[:, cix * 128:(cix + 1) * 128], z1tok[b][:, cix * 128:(cix + 1) * 128],
                            ident[:], start=True, stop=True),
                            inc=("z1_mm" if cix == NCH - 1 else None))
                    P.wait("vector", "z1_mm")
                    for g in range(4):
                        P.op("vector", lambda E, g=g, qc=qc: E.tensor_copy(
                            avt[:, :].rearrange("p (c s) -> p c s", c=NCH)[:, 4 * g:4 * g + 4, qc * 128:(qc + 1) * 128],
                            ps_z[:, g * 512:(g + 1) * 512].rearrange("p (c s) -> p c s", c=4)),
                            inc=("z1_ev" if g == 3 else None))

                for qc in range(nqc + 1):
                    if qc < nqc:
                        front(qc)
                    if qc >= 1:
                        back(qc - 1)

            return p3

        def make_p4a(hp, c4):
            sb4 = lambda nm, sh, dt: c4.enter_context(nc.sbuf_tensor(nm + f"_h{hp}", sh, dt))
            sgb = sb4("sgb", [128, hid], BF16)
            sub_ = sb4("sub", [128, hid], BF16)
            sdb = sb4("sdb", [128, D], BF16)
            srow = sb4("srow", [1, 1024], F32)
            srow_bf = sb4("srow_bf", [1, 1024], BF16)
            wi_stage = [sb4(f"wi_stage{i}", [128, D], I32) for i in range(4)]
            wf_tmp = [sb4(f"wf_tmp{i}", [128, D], F32) for i in range(3)]
            w_g = [sb4(f"w_g{i}", [128, D], BF16) for i in range(3)]
            w_u = [sb4(f"w_u{i}", [128, D], BF16) for i in range(3)]
            wdi2 = [sb4(f"wdi2_{i}", [128, D], I32) for i in range(2)]
            wdc = [sb4(f"wdc{i}", [128, D], BF16) for i in range(1)]
            gate_t = sb4("gate_t", [128, 512], BF16)
            up_t = sb4("up_t", [128, 512], BF16)
            gu_t = [sb4(f"gu_t{i}", [128, 512], BF16) for i in range(2)]
            ps_g = [c4.enter_context(nc.psum_tensor(f"ps_g{i}_h{hp}", [128, 512], F32)) for i in range(3)]
            ps_u = [c4.enter_context(nc.psum_tensor(f"ps_u{i}_h{hp}", [128, 512], F32)) for i in range(3)]
            ps_b = c4.enter_context(nc.psum_tensor(f"ps_b_h{hp}", [128, 512], F32))

            def bcast_scales(P):
                for s_d_t, dst, n in [(sg_d, sgb, hid), (su_d, sub_, hid), (sd_d, sdb, D)]:
                    for off in range(0, n, 1024):
                        ln = min(1024, n - off)
                        P.wait("sync", "sb_mm", P.C["sb_mm"])
                        P.dma("sync", srow[0:1, 0:ln], s_d_t[0:1, off:off + ln], inc="sb_ld")
                        P.wait("vector", "sb_ld")
                        P.wait("vector", "sb_mm", P.C["sb_mm"])
                        P.op("vector", lambda E, ln=ln: E.tensor_copy(srow_bf[0:1, 0:ln], srow[0:1, 0:ln]), inc="sb_cast")
                        P.wait("tensor", "sb_cast")
                        for t in range(ln // 512):
                            P.wait("tensor", "sb_ev")
                            P.op("tensor", lambda E, t=t: E.matmul(
                                ps_b[:], ones_row[:], srow_bf[0:1, t * 512:(t + 1) * 512],
                                start=True, stop=True), inc="sb_mm")
                            P.wait("vector", "sb_mm")
                            P.op("vector", lambda E, t=t, off=off, dst=dst: E.tensor_copy(
                                dst[:, off + t * 512:off + (t + 1) * 512], ps_b[:]), inc="sb_ev")

            def p4a(P):
                bcast_scales(P)
                SB_EV = P.C["sb_ev"]
                WP_BASE = P.C["wp_sc"]

                def prep(hc):
                    for j3, (w_d, w_bf2, sb_t) in enumerate([(wgt_d, w_g, sgb), (wut_d, w_u, sub_)]):
                        it = hc * 3 + j3
                        wi = (hc * 2 + j3) % 4
                        wf = it % 3
                        b = hc % 3
                        P.wait("sync", "wp_shift", P.C["wp_shift"] - 3)
                        P.dma("sync", wi_stage[wi][:], w_d[hc, :, :], inc=f"wp_ld{wi}")
                        P.wait("scalar", f"wp_ld{wi}")
                        P.wait("scalar", "wp_sc", P.C["wp_sc"] - 2)
                        P.op("scalar", lambda E, wi=wi, wf=wf: E.activation(wf_tmp[wf][:], wi_stage[wi][:], AF.Copy, bias=-128.0),
                             inc="wp_shift")
                        P.wait("vector", "wp_shift")
                        if hc == 0 and j3 == 0:
                            P.wait("vector", "sb_ev", SB_EV)
                        P.wait("vector", "u_mm", P.C["u_mm"])
                        P.op("vector", lambda E, b=b, hc=hc, wf=wf, w_bf2=w_bf2, sb_t=sb_t: E.tensor_tensor(
                            w_bf2[b][:].rearrange("p (c e) -> p c e", c=NCH),
                            wf_tmp[wf][:].rearrange("p (c e) -> p c e", c=NCH),
                            sb_t[:, None, hc * 128:(hc + 1) * 128].broadcast_to([128, NCH, 128]),
                            op=ALU.mult), inc="wp_sc")
                    it = hc * 3 + 2
                    b2 = 0
                    wf = it % 3
                    bi = hc % 2
                    P.wait("sync", "wd_shift", P.C["wd_shift"] - 1)
                    P.dma("sync", wdi2[bi][:], wdt_d[hc * 128:(hc + 1) * 128, :], inc=f"wd_ld{bi}")
                    P.wait("scalar", f"wd_ld{bi}")
                    P.wait("scalar", "wp_sc", P.C["wp_sc"] - 2)
                    P.op("scalar", lambda E, bi=bi, wf=wf: E.activation(wf_tmp[wf][:], wdi2[bi][:], AF.Copy, bias=-128.0),
                         inc="wd_shift")
                    P.wait("vector", "wd_shift")
                    P.wait("vector", f"wd_st{b2}")
                    P.op("vector", lambda E, b2=b2, wf=wf: E.tensor_tensor(
                        wdc[b2][:], wf_tmp[wf][:], sdb[:], op=ALU.mult), inc="wp_sc")
                    P.wait("gpsimd", "wp_sc")
                    for et in range(4):
                        P.dma("gpsimd", wd_s[et, :, hc * 512:(hc + 1) * 512],
                              wdc[b2][:, et * 512:(et + 1) * 512], inc=f"wd_st{b2}")

                prep(0)
                prep(1)
                for hc in range(nhc):
                    b = hc % 3
                    if hc + 2 < nhc:
                        prep(hc + 2)
                    for qt in range(nqt):
                        pp = (hc * nqt + qt) % 3
                        P.wait("tensor", "wp_sc", WP_BASE + 3 * hc + 2)
                        P.wait("tensor", "g_act", P.C["g_act"] - 2)
                        P.wait("tensor", "u_cp", P.C["u_cp"] - 2)
                        for cix in range(NCH):
                            P.op("tensor", lambda E, cix=cix, pp=pp, qt=qt, b=b: E.matmul(
                                ps_g[pp][:],
                                w_g[b][:, cix * 128:(cix + 1) * 128],
                                avt[:, cix * S_half + qt * 512: cix * S_half + qt * 512 + 512],
                                start=(cix == 0), stop=(cix == NCH - 1)),
                                inc=("g_mm" if cix == NCH - 1 else None))
                        for cix in range(NCH):
                            P.op("tensor", lambda E, cix=cix, pp=pp, qt=qt, b=b: E.matmul(
                                ps_u[pp][:],
                                w_u[b][:, cix * 128:(cix + 1) * 128],
                                avt[:, cix * S_half + qt * 512: cix * S_half + qt * 512 + 512],
                                start=(cix == 0), stop=(cix == NCH - 1)),
                                inc=("u_mm" if cix == NCH - 1 else None))
                        P.wait("scalar", "g_mm")
                        P.wait("scalar", "gu_mul", P.C["gu_mul"])
                        P.op("scalar", lambda E, pp=pp: E.activation(gate_t[:], ps_g[pp][:], AF.Silu),
                             inc="g_act")
                        P.wait("vector", "u_mm")
                        P.wait("vector", "gu_mul", P.C["gu_mul"])
                        P.op("vector", lambda E, pp=pp: E.tensor_copy(up_t[:], ps_u[pp][:]), inc="u_cp")
                        gb = (hc * nqt + qt) % 2
                        P.wait("vector", "g_act")
                        P.wait("vector", "u_cp")
                        P.wait("vector", f"gu_st{gb}")
                        P.op("vector", lambda E, gb=gb: E.tensor_tensor(
                            gu_t[gb][:], gate_t[:], up_t[:], op=ALU.mult), inc="gu_mul")
                        P.wait("gpsimd", "gu_mul")
                        P.dma("gpsimd", gu_s[qt, :, hc * 512:(hc + 1) * 512], gu_t[gb][:], inc=f"gu_st{gb}")
                P.wait("gpsimd", "gu_st0")
                P.wait("gpsimd", "gu_st1")
                P.wait("gpsimd", "wd_st0")
                P.wait("gpsimd", "wd_st1")

            return p4a

        def make_p4b(hp, c4):
            sb4 = lambda nm, sh, dt: c4.enter_context(nc.sbuf_tensor(nm + f"_h{hp}", sh, dt))
            wd_bf = [sb4(f"wd_bf{i}", [128, (nhc // 2) * 512], BF16) for i in range(3)]
            gu_sl = [sb4(f"gu_sl{i}", [128, (nhc // 4) * 512], BF16) for i in range(3)]
            r2st = [sb4(f"r2st{i}", [128, 512], F32) for i in range(4)]
            ps_d = [c4.enter_context(nc.psum_tensor(f"ps_d{i}_h{hp}", [128, 512], F32)) for i in range(8)]
            nsl = nhc // 4
            nhf = nhc // 2

            def p4b(P):
                for et in range(4):
                    for half in range(2):
                        h5 = et * 2 + half
                        P.wait("sync", "d_half", P.C["d_half"] - (16 if half == 0 else 2))
                        P.dma("sync", wd_bf[h5 % 3][:],
                              wd_s[et, :, half * nhf * 512:(half + 1) * nhf * 512],
                              inc=f"wdb_ld{h5 % 3}")
                    for qt in range(nqt):
                        it3 = et * nqt + qt
                        dset = (it3 % 2) * 4
                        P.wait("tensor", "r2_ev", P.C["r2_ev"] - 1)
                        for u in range(4):
                            for g in range(4):
                                cix = et * 4 + g
                                P.op("tensor", lambda E, u=u, g=g, cix=cix, qt=qt, dset=dset: E.matmul(
                                    ps_d[dset + u][:, g * 128:(g + 1) * 128],
                                    avt[:, cix * S_half + qt * 512 + u * 128: cix * S_half + qt * 512 + (u + 1) * 128],
                                    ident[:], start=(g == 0), stop=False),
                                    inc=None)
                        for sl in range(4):
                            g3 = it3 * 4 + sl
                            h5 = et * 2 + sl // 2
                            P.wait("sync", "d_half", P.C["d_half"] - 2)
                            P.dma("sync", gu_sl[g3 % 3][:],
                                  gu_s[qt, :, sl * nsl * 512:(sl + 1) * nsl * 512],
                                  inc=f"gu_ld{g3 % 3}")
                            P.wait("tensor", f"gu_ld{g3 % 3}")
                            P.wait("tensor", f"wdb_ld{h5 % 3}")
                            for ci in range(nsl):
                                hc = sl * nsl + ci
                                for u in range(4):
                                    P.op("tensor", lambda E, ci=ci, u=u, hc=hc, g3=g3, h5=h5, dset=dset: E.matmul(
                                        ps_d[dset + u][:],
                                        gu_sl[g3 % 3][:, ci * 512 + u * 128: ci * 512 + (u + 1) * 128],
                                        wd_bf[h5 % 3][:, (hc - (h5 % 2) * nhf) * 512:(hc - (h5 % 2) * nhf + 1) * 512],
                                        start=False, stop=(hc == nhc - 1)),
                                        inc=("d_half" if (ci == nsl - 1 and u == 3) else None))
                        P.wait("vector", "d_half")
                        P.wait("vector", "r2_st")
                        for u in range(4):
                            P.op("vector", lambda E, u=u, dset=dset: E.tensor_copy(r2st[u][:], ps_d[dset + u][:]),
                                 inc=("r2_ev" if u == 3 else None))
                        P.wait("gpsimd", "r2_ev")
                        for u in range(4):
                            P.dma("gpsimd", r2_s[qt * 512 + u * 128: qt * 512 + (u + 1) * 128, et * 512:(et + 1) * 512],
                                  r2st[u][:], inc="r2_st")
                P.wait("gpsimd", "r2_st")

            return p4b

        def make_ln2(hp, c4):
            sb4 = lambda nm, sh, dt: c4.enter_context(nc.sbuf_tensor(nm + f"_l{hp}", sh, dt))
            r2 = [sb4(f"r2_{i}", [128, D], F32) for i in range(2)]
            r2sq = sb4("r2sq", [128, D], F32)
            o_t = [sb4(f"o_t{i}", [128, D], F32) for i in range(2)]
            st_ln = [sb4(f"st_ln2{i}", [128, 8], F32) for i in range(2)]

            def emit_chunk(P, qc):
                b = qc % 2
                sl = st_ln[b]
                P.wait("sync", "o_done", P.C["o_done"] - 1)
                P.dma("sync", r2[b][:], r2_s[qc * 128:(qc + 1) * 128, :], inc=f"r2_ld{b}")
                P.wait("scalar", f"r2_ld{b}")
                P.wait("scalar", "ln2_mid", P.C["ln2_mid"] - 2)
                P.wait("scalar", "r2sq_done", P.C["r2sq_done"] - 1)
                P.op("scalar", lambda E, b=b, sl=sl: E.activation(r2sq[:], r2[b][:], AF.Copy, accum_out=sl[:, 0:1]),
                     inc="r2sq_done")
                P.wait("scalar", "r2sq_done")
                P.op("scalar", lambda E, b=b, sl=sl: E.activation(r2sq[:], r2[b][:], AF.Square, accum_out=sl[:, 1:2]),
                     inc="ln2_sum")
                P.wait("vector", "ln2_sum")
                P.op("vector", lambda E, sl=sl: E.tensor_scalar(sl[:, 2:3], sl[:, 0:1], 1.0 / D, None, op0=ALU.mult), inc="ln2_mid")
                P.wait("vector", "ln2_mid")
                P.op("vector", lambda E, sl=sl: E.tensor_tensor(sl[:, 3:4], sl[:, 2:3], sl[:, 2:3], op=ALU.mult), inc="ln2_mid")
                P.wait("vector", "ln2_mid")
                P.wait("vector", "ln2_sqrt", P.C["ln2_sqrt"] - 1)
                P.op("vector", lambda E, sl=sl: E.scalar_tensor_tensor(
                    sl[:, 4:5], sl[:, 1:2], 1.0 / D, sl[:, 3:4], op0=ALU.mult, op1=ALU.subtract),
                    inc="ln2_var")
                P.wait("scalar", "ln2_var")
                P.op("scalar", lambda E, sl=sl: E.activation(sl[:, 5:6], sl[:, 4:5], AF.Sqrt, bias=eps_col[:]),
                     inc="ln2_sqrt")
                P.wait("vector", "ln2_sqrt")
                P.op("vector", lambda E, sl=sl: E.reciprocal(sl[:, 6:7], sl[:, 5:6]), inc="ln2_rstd")
                P.wait("vector", "ln2_rstd")
                P.op("vector", lambda E, sl=sl: E.scalar_tensor_tensor(
                    sl[:, 7:8], sl[:, 2:3], -1.0, sl[:, 6:7], op0=ALU.mult, op1=ALU.mult),
                    inc="ln2_nmr")
                P.wait("vector", "ln2_nmr")
                P.wait("vector", f"o_st{b}")
                P.op("vector", lambda E, b=b, sl=sl: E.tensor_scalar(
                    o_t[b][:], r2[b][:], sl[:, 6:7], sl[:, 7:8], op0=ALU.mult, op1=ALU.add),
                    inc="o_done")
                P.wait("gpsimd", "o_done")
                P.dma("gpsimd", out_d[hp * S_half + qc * 128: hp * S_half + (qc + 1) * 128, :],
                      o_t[b][:], inc=f"o_st{b}")

            def finish(P):
                P.wait("gpsimd", "o_st0")
                P.wait("gpsimd", "o_st1")

            return emit_chunk, finish

        for hp in range(2):
            with ExitStack() as c2:
                if hp == 1:
                    ln2_emit0, ln2_fin0 = make_ln2(0, c2)
                    p2 = make_p2(hp, c2, tail=ln2_emit0)
                    P.phase(f"p2_h{hp}", lambda P: (p2(P), ln2_fin0(P)))
                else:
                    p2 = make_p2(hp, c2)
                    P.phase(f"p2_h{hp}", p2)
            if debug_stop == "p2":
                with ExitStack() as cd:
                    st = cd.enter_context(nc.sbuf_tensor("avdbg_st", [128, S_half], BF16))

                    def dump(P):
                        for cix in range(NCH):
                            P.wait("vector", "d_st")
                            P.op("vector", lambda E, cix=cix: E.tensor_copy(
                                st[:], avt[:, cix * S_half:(cix + 1) * S_half]), inc="d_cp")
                            P.wait("gpsimd", "d_cp")
                            P.dma("gpsimd", avt_dbg[cix * 128:(cix + 1) * 128, :], st[:], inc="d_st")
                        P.wait("gpsimd", "d_st")
                    P.phase("dump", dump)
                return nc
            with ExitStack() as c3:
                p3 = make_p3(hp, c3)
                P.phase(f"p3_h{hp}", p3)
            if debug_stop == "p3":
                with ExitStack() as cd:
                    st = cd.enter_context(nc.sbuf_tensor("z1dbg_st", [128, S_half], BF16))

                    def dump(P):
                        for cix in range(NCH):
                            P.wait("vector", "d_st")
                            P.op("vector", lambda E, cix=cix: E.tensor_copy(
                                st[:], avt[:, cix * S_half:(cix + 1) * S_half]), inc="d_cp")
                            P.wait("gpsimd", "d_cp")
                            P.dma("gpsimd", z1t_dbg[cix * 128:(cix + 1) * 128, :], st[:], inc="d_st")
                        P.wait("gpsimd", "d_st")
                    P.phase("dump", dump)
                return nc
            with ExitStack() as c4:
                p4a = make_p4a(hp, c4)
                P.phase(f"p4a_h{hp}", p4a)
            with ExitStack() as c4:
                p4b = make_p4b(hp, c4)
                P.phase(f"p4b_h{hp}", p4b)
            if debug_stop == "p4b" and hp == 0:
                return nc
            if hp == 1:
                with ExitStack() as c4:
                    ln2_emit1, ln2_fin1 = make_ln2(1, c4)

                    def p4c(P):
                        for qc in range(nqc):
                            ln2_emit1(P, qc)
                        ln2_fin1(P)
                    P.phase("p4c_h1", p4c)

        return nc
    return nc


_T = lambda a: np.ascontiguousarray(np.asarray(a).T)


def host_layout(inputs, hid=HID):
    L = {}
    L["wqt"] = _T(inputs["L0_attn_Wq"])
    L["wkt"] = _T(inputs["L0_attn_Wk"])
    L["wvt"] = _T(inputs["L0_attn_Wv"])
    L["wot"] = _T(inputs["L0_attn_Wo"])
    L["bq"] = _T(np.asarray(inputs["L0_attn_bq"], dtype=np.float32).reshape(NCH, 128))
    L["bk"] = _T(np.asarray(inputs["L0_attn_bk"], dtype=np.float32).reshape(NCH, 128))
    L["bv"] = _T(np.asarray(inputs["L0_attn_bv"], dtype=np.float32).reshape(NCH, 128))
    L["bo_row"] = np.ascontiguousarray(np.asarray(inputs["L0_attn_bo"], dtype=np.float32).reshape(1, D))
    wg = np.asarray(inputs["L0_ffn_wq_gate"])
    wu = np.asarray(inputs["L0_ffn_wq_up"])
    L["wgt_r"] = np.ascontiguousarray(
        wg.reshape(hid // 128, 128, NCH, 128).transpose(0, 3, 2, 1).reshape(hid // 128, 128, NCH * 128))
    L["wut_r"] = np.ascontiguousarray(
        wu.reshape(hid // 128, 128, NCH, 128).transpose(0, 3, 2, 1).reshape(hid // 128, 128, NCH * 128))
    L["wdt"] = _T(inputs["L0_ffn_wq_down"])
    L["sg_row"] = np.ascontiguousarray(np.asarray(inputs["L0_ffn_s_gate"], dtype=np.float32).reshape(1, hid))
    L["su_row"] = np.ascontiguousarray(np.asarray(inputs["L0_ffn_s_up"], dtype=np.float32).reshape(1, hid))
    L["sd_row"] = np.ascontiguousarray(np.asarray(inputs["L0_ffn_s_down"], dtype=np.float32).reshape(1, D))
    L["ident"] = np.eye(128, dtype=np.float32)
    L["ones_in"] = np.ones((128, 128), dtype=np.float32)
    mask = np.zeros((128, 512), dtype=np.float32)
    qq = np.arange(256)
    for j in range(2):
        for kk in range(128):
            mask[kk, j * 256:(j + 1) * 256] = (qq >= j * 128 + kk).astype(np.float32)
    L["mask"] = mask
    return L


def kernel(**inputs):
    X = np.asarray(inputs["L0_attn_X"], dtype=np.float32)
    B, S, d = X.shape
    assert d == D
    L = host_layout(inputs)
    nc = build_program(S=S)
    in_maps = []
    for c in range(B):
        m = dict(L)
        m["xtok"] = np.ascontiguousarray(X[c])
        m["xt"] = _T(X[c])
        in_maps.append(m)
    res = run_bass_kernel_spmd(nc, in_maps, core_ids=list(range(B)))
    return np.stack([res.results[c]["out"] for c in range(B)], axis=0)
